# version 20
# baseline (speedup 1.0000x reference)
import os

import numpy as np
import concourse.bass as bass
import concourse.bacc as bacc
import concourse.mybir as mybir
import concourse.dve_ops as DOPS
from concourse.dve_op import DveOp
from concourse.dve_spec import Spec, Src0, Src1, C0, C2, sq, select, lower
from concourse.bass_utils import run_bass_kernel_spmd

F32 = mybir.dt.float32
BF16 = mybir.dt.bfloat16
F8 = mybir.dt.float8e4
DR = mybir.MatmulPerfMode.DoubleRow
AF = mybir.ActivationFunctionType
ALU = mybir.AluOpType
AX = mybir.AxisListType

NB = 2
TT = 16
T = TT * 128
D = 1024
DC = D // 128
NH = 16
NP = NH // 2
HF = 4096
HT = HF // 128
QH = 1024
NCORES = 8

C_SCALE8, C_NEGLN8, C_HALF, C_NEGHALF, C_EPS, C_NEG1 = range(6)
NCST = 8


def _make_consts():
    c = np.zeros((128, NCST), dtype=np.float32)
    c[:, C_SCALE8] = 0.125
    c[:, C_NEGLN8] = np.float32(-np.log(8.0))
    c[:, C_HALF] = 0.5
    c[:, C_NEGHALF] = -0.5
    c[:, C_EPS] = 1e-5
    c[:, C_NEG1] = -1.0
    return c


_EXP2_HEX = [
    '0x1.0000000000008p-60', '0x1.0000000000011p-59', '0x1.ffffffffffff2p-59',
    '0x1.0000000000001p-57', '0x1.000000000000ap-56', '0x1.0000000000012p-55',
    '0x1.ffffffffffff5p-55', '0x1.0000000000003p-53', '0x1.000000000000bp-52',
    '0x1.0000000000014p-51', '0x1.ffffffffffff8p-51', '0x1.0000000000005p-49',
    '0x1.000000000000dp-48', '0x1.fffffffffffebp-48', '0x1.ffffffffffffcp-47',
    '0x1.0000000000006p-45', '0x1.ffffffffffffdp-45', '0x1.0000000000007p-43',
    '0x1.fffffffffffffp-43', '0x1.0000000000008p-41', '0x1.0000000000000p-40',
    '0x1.0000000000009p-39', '0x1.0000000000001p-38', '0x1.0000000000009p-37',
    '0x1.0000000000002p-36', '0x1.000000000000ap-35', '0x1.0000000000003p-34',
    '0x1.000000000000bp-33', '0x1.0000000000003p-32', '0x1.ffffffffffff7p-32',
    '0x1.0000000000004p-30', '0x1.ffffffffffff9p-30', '0x1.0000000000005p-28',
    '0x1.ffffffffffffbp-28', '0x1.0000000000006p-26', '0x1.ffffffffffffcp-26',
    '0x1.0000000000007p-24', '0x1.ffffffffffffep-24', '0x1.fffffffffffffp-23',
    '0x1.fffffffffffffp-22', '0x1.0000000000000p-20', '0x1.0000000000000p-19',
    '0x1.0000000000001p-18', '0x1.0000000000001p-17', '0x1.0000000000002p-16',
    '0x1.0000000000002p-15', '0x1.0000000000002p-14', '0x1.0000000000003p-13',
    '0x1.0000000000003p-12', '0x1.fffffffffffffp-12', '0x1.0000000000000p-10',
    '0x1.0000000000000p-9', '0x1.0000000000001p-8', '0x1.0000000000001p-7',
    '0x1.0000000000002p-6', '0x1.0000000000000p-5', '0x1.0000000000000p-4',
    '0x1.0000000000001p-3', '0x1.0000000000000p-2', '0x1.0000000000000p-1',
    '0x1.0000000000000p+0', '0x1.0000000000000p+1', '0x1.0000000000000p+2',
    '0x1.ffffffffffffep+2', '0x1.fffffffffffffp+3', '0x1.0000000000000p+5',
    '0x1.ffffffffffffdp+5', '0x1.ffffffffffffep+6', '0x1.ffffffffffffep+7',
    '0x1.fffffffffffffp+8', '0x1.0000000000000p+10', '0x1.0000000000000p+11',
    '0x1.ffffffffffff9p+11', '0x1.ffffffffffffap+12', '0x1.ffffffffffffbp+13',
    '0x1.ffffffffffffcp+14', '0x1.ffffffffffffdp+15', '0x1.ffffffffffffdp+16',
    '0x1.ffffffffffffep+17', '0x1.fffffffffffffp+18', '0x1.0000000000000p+20',
    '0x1.0000000000000p+21', '0x1.0000000000001p+22', '0x1.0000000000001p+23',
    '0x1.ffffffffffff3p+23', '0x1.0000000000002p+25', '0x1.ffffffffffff5p+25',
    '0x1.0000000000003p+27', '0x1.ffffffffffff6p+27', '0x1.0000000000003p+29',
    '0x1.ffffffffffff8p+29', '0x1.0000000000004p+31', '0x1.ffffffffffff9p+31',
    '0x1.fffffffffffeap+32', '0x1.ffffffffffffbp+33', '0x1.fffffffffffecp+34',
    '0x1.ffffffffffffcp+35', '0x1.fffffffffffedp+36', '0x1.ffffffffffffep+37',
    '0x1.fffffffffffefp+38', '0x1.0000000000000p+40',
]
_EXP2_LUT = np.array([float.fromhex(h) for h in _EXP2_HEX])


def _np_float_quantize(x, exp_bits, man_bits):
    xd = np.asarray(x).astype(np.float64)
    bias = 2 ** (exp_bits - 1) - 1
    min_e = 1 - bias
    _, e = np.frexp(xd)
    ue = np.maximum(e - 1, min_e)
    kk = (ue - man_bits).astype(np.int64)
    kc = np.clip(kk, -60, 40)
    scale = _EXP2_LUT[kc + 60]
    scale = np.where(kk == kc, scale, np.exp2(kk.astype(np.float64)))
    q = np.round(xd / scale) * scale
    return np.where(xd == 0, np.zeros_like(xd), q)


_QPARAMS = {
    "ANTQ_E4M3": (4, 3, 20, 1.5 * 2 ** 14, 2.0 ** -6),
    "ANTQ_FP16": (5, 10, 13, 0.75, 2.0 ** -14),
}
_QREG = {}


def _register_q(name):
    if name in _QREG:
        return _QREG[name]
    for op in DOPS.OPS:
        if op.name == name:
            _QREG[name] = op
            return op
    EW, MW, s, K, thresh = _QPARAMS[name]
    grid = 2.0 ** (1 - (2 ** (EW - 1) - 1) - MW)
    t = Src0 * C0
    body = select(sq(Src0) >= C2, t - (t - Src0), Src1)

    def ref(in0, in1, s0, s1, imm2, _EW=EW, _MW=MW):
        return _np_float_quantize(in0, _EW, _MW)

    spec = Spec(body=body)
    shas = {}
    for ver in ("v3", "v4"):
        try:
            shas[ver] = lower(spec, ver=ver, perf_en=False, name=name).sha(ver)
        except Exception:
            pass
    op = DveOp(name, spec, subdim=False, uops_sha=shas, reference=ref)
    DOPS.OPS.append(op)
    DOPS.CUSTOM_DVE_SPECS[name] = spec
    DOPS.CUSTOM_DVE_REFS[name] = ref
    DOPS._SUB_OPCODE_FOR_NAME[name] = DOPS._CUSTOM_DVE_ROW_BASE + len(DOPS.OPS) - 1
    assert DOPS._SUB_OPCODE_FOR_NAME[name] < 0x20
    _QREG[name] = op
    return op


class Res:
    __slots__ = ("name", "writer", "readers")

    def __init__(self, name=""):
        self.name = name
        self.writer = None
        self.readers = []


class _Rec:
    __slots__ = ("calls",)

    def __init__(self):
        self.calls = []

    def __getattr__(self, name):
        def f(*a, **k):
            self.calls.append((name, a, k))
        return f


class Em:
    ENG = ("pe", "act", "dve", "pool", "sp")

    def __init__(self, nc):
        self.nc = nc
        self._ctx = []
        self.sems = {}
        self.cnt = {}
        self.waited = {e: {} for e in self.ENG}
        self.prog = {e: [] for e in self.ENG}
        for q in self.ENG:
            self._new_sem(q)

    def _new_sem(self, q):
        cm = self.nc.semaphore("sem_" + q)
        self.sems[q] = cm.__enter__()
        self._ctx.append(cm)
        self.cnt[q] = 0

    def dma_sem(self, q):
        if q not in self.sems:
            self._new_sem(q)
        return q

    def sbuf(self, name, shape, dt=F32):
        cm = self.nc.sbuf_tensor(name, list(shape), dt)
        t = cm.__enter__()
        self._ctx.append(cm)
        return t, Res(name)

    def op(self, e, build, ins=(), outs=(), dma=None):
        need = {}
        for r in ins:
            if r.writer:
                s, c = r.writer
                need[s] = max(need.get(s, 0), c)
        for r in outs:
            if r.writer:
                s, c = r.writer
                need[s] = max(need.get(s, 0), c)
            for s, c in r.readers:
                need[s] = max(need.get(s, 0), c)
        w = self.waited[e]
        waits = [(s, c) for s, c in sorted(need.items()) if w.get(s, 0) < c]
        for s, c in waits:
            w[s] = c
        if dma is not None:
            self.cnt[dma] += 16
            tok = (dma, self.cnt[dma])
            inc = (dma, 16)
        else:
            self.cnt[e] += 1
            tok = (e, self.cnt[e])
            inc = (e, 1)
        rec = _Rec()
        build(rec)

        def bound(E, _calls=rec.calls):
            inst = None
            for name, a, k in _calls:
                inst = getattr(E, name)(*a, **k)
            return inst

        self.prog[e].append((waits, bound, inc))
        for r in outs:
            r.writer = tok
            r.readers = []
        for r in ins:
            r.readers.append(tok)
        return tok

    def wait(self, e, q, cnt):
        if self.waited[e].get(q, 0) >= cnt:
            return
        self.prog[e].append(([(q, cnt)], None, None))
        self.waited[e][q] = cnt

    def flush(self):
        with self.nc.Block() as block:
            def emit_stream(e, E):
                for waits, build, inc in self.prog[e]:
                    for s, c in waits:
                        E.wait_ge(self.sems[s], c)
                    if build is not None:
                        inst = build(E)
                        if inc is not None:
                            inst.then_inc(self.sems[inc[0]], inc[1])

            @block.tensor
            def _(E):
                emit_stream("pe", E)

            @block.scalar
            def _(E):
                emit_stream("act", E)

            @block.vector
            def _(E):
                emit_stream("dve", E)

            @block.gpsimd
            def _(E):
                emit_stream("pool", E)

            @block.sync
            def _(E):
                emit_stream("sp", E)

    def close(self):
        for cm in reversed(self._ctx):
            cm.__exit__(None, None, None)


def emit_quant(em, opname, out_ap, in_ap, ins, outs, scr_ap, scr_res, e="dve"):
    op = _register_q(opname)
    EW, MW, s, K, thresh = _QPARAMS[opname]
    Cc = float(2.0 ** s + 1.0)
    em.op(e, lambda E: E.tensor_scalar(scr_ap, in_ap, float(K), float(K),
                                       ALU.add, ALU.subtract),
          ins=list(ins), outs=[scr_res])
    return em.op(e, lambda E: E._custom_dve(op, out=out_ap, in0=in_ap,
                                            in1=scr_ap, s0=Cc,
                                            imm2=float(thresh * thresh)),
                 ins=list(ins) + [scr_res], outs=list(outs))


def emit_quant_actsnap(em, opname, out_ap, in_ap, ins, outs, scr_ap, scr_res):
    op = _register_q(opname)
    EW, MW, s, K, thresh = _QPARAMS[opname]
    Cc = float(2.0 ** s + 1.0)
    em.op("act", lambda E: E.activation(scr_ap, in_ap, AF.Copy,
                                        bias=float(K), scale=1.0),
          ins=list(ins), outs=[scr_res])
    em.op("act", lambda E: E.activation(scr_ap, scr_ap, AF.Copy,
                                        bias=float(-K), scale=1.0),
          ins=[scr_res], outs=[scr_res])
    return em.op("dve", lambda E: E._custom_dve(op, out=out_ap, in0=in_ap,
                                                in1=scr_ap, s0=Cc,
                                                imm2=float(thresh * thresh)),
                 ins=list(ins) + [scr_res], outs=list(outs))


def build_program(nc):
    din = lambda n, s, dt=F32: nc.dram_tensor(n, list(s), dt,
                                              kind="ExternalInput").ap()
    x_in = din("x", [NB, T, D])
    mask_in = din("mk", [NB, T, D], BF16)
    Wqkv = din("wqkv", [D, 3 * D])
    bqkv_in = din("bqkv", [1, 3 * D])
    Wproj = din("wproj", [D, D])
    bproj_in = din("bproj", [1, D])
    w1T = din("w1t", [D, HF])
    b1_in = din("b1", [HT, 128])
    w2T = din("w2t", [HF, D])
    b2_in = din("b2", [1, D])
    gamma_in = din("gam", [DC, 128])
    beta_in = din("bet", [DC, 128])
    ident_in = din("idn", [128, 128], BF16)
    identf_in = din("idf", [128, 128])
    cst_in = din("cst", [128, NCST])

    w1q_sc = nc.dram_tensor("w1qs", [D, HF], F8, kind="Internal").ap()
    w2q_sc = nc.dram_tensor("w2qs", [HF, D], F8, kind="Internal").ap()
    out_hbm = nc.dram_tensor("out", [NB, T, D], F32, kind="ExternalOutput").ap()

    em = Em(nc)
    S = em.sbuf

    pcm = nc.psum_tensor("pp", [128, 3584], F32)
    pp = pcm.__enter__()
    em._ctx.append(pcm)
    pbcm = nc.psum_tensor("ppb", [128, 1024], BF16)
    ppb = pbcm.__enter__()
    em._ctx.append(pbcm)
    pbank = [Res(f"pb{i}") for i in range(7)]
    ppb_r = Res("ppb")

    def pb(lo, hi):
        return [pbank[i] for i in range(lo // 512, (hi + 511) // 512)]

    A, A_r = S("arnA", (128, 16384), BF16)
    B, B_r = S("arnB", (128, 16384), BF16)
    C, C_r = S("arnC", (128, 16640), BF16)
    W, W_r = S("arnW", (128, 24576), BF16)
    Whi_r = Res("arnWhi")
    FA, FA_r = S("arnF", (128, 4096))

    def addr_of(name):
        for a in nc.m.functions[0].allocations:
            mls = getattr(a, "memorylocations", None)
            if not mls:
                continue
            if mls[0].name == name:
                return mls[0].addr
        raise KeyError(name)

    qk_sb = nc.alloc_sbuf_tensor_at("qksb", [128, 2048], BF16,
                                    offset=addr_of("arnF") + 12288)
    yfull = nc.alloc_sbuf_tensor_at("yfull", [128, 1024], F32,
                                    offset=addr_of("arnA") + 16384)
    yq = nc.alloc_sbuf_tensor_at("yq", [128, 1024], F32,
                                 offset=addr_of("arnA") + 20480)
    bproj_sb = nc.alloc_sbuf_tensor_at("bprojs", [1, 1024], F32,
                                       offset=addr_of("arnC") + 20480)
    b2q = nc.alloc_sbuf_tensor_at("b2q", [1, 1024], F32,
                                  offset=addr_of("arnC") + 24576)
    qk_sb_r = Res("qksb")
    yfull_r = Res("yfull")
    yq_r = Res("yq")
    bproj_r = Res("bprojs")
    b2q_r = Res("b2q")

    QTb = A[:, 0:NP * T]
    xn2qT = A[:, 0:DC * 512]
    KTb = B[:, 0:NP * T]
    hTb = B[:, 0:HT * 512]
    Vb = C[:, 0:TT * NH * 65]
    wp = lambda c: C[:, c * 1024:(c + 1) * 1024]
    w2b = lambda i: C[:, 8192 + i * 1024:8192 + (i + 1) * 1024]
    wq = lambda c: W[:, c * 3072:(c + 1) * 3072]

    def wq_res(c):
        lo, hi = c * 3072, (c + 1) * 3072
        rs = []
        if lo < 16384:
            rs.append(W_r)
        if hi > 16384:
            rs.extend(whiR)
        return rs

    ctx_all = W[:, 0:NP * T]
    w1h = W[:, 16384:24576]
    wf = FA[:, 0:3072]
    x2sl = lambda i: FA[:, (i % 4) * 1024:((i % 4) + 1) * 1024]

    cst, cst_r = S("cstt", (128, NCST))
    ident, ident_r = S("ident", (128, 128), BF16)
    identf, identf_r = S("identf", (128, 128))
    ones128b, ones128b_r = S("ones128b", (128, 1), BF16)
    onesb1, onesb1_r = S("onesb1", (1, 128), BF16)
    bcombb, bcombb_r = S("bcombb", (1, 3072), BF16)
    bprojb, bprojb_r = S("bprojb", (1, 1024), BF16)
    b2qb, b2qb_r = S("b2qb", (1, 1024), BF16)
    recipb, recipb_r = S("recipb", (1, 1024), BF16)
    qqkkb, qqkkb_r = S("qqkkb", (1, NH), BF16)
    gam, gam_r = S("gamt", (128, DC))
    bet, bet_r = S("bett", (128, DC))
    b1q, b1q_r = S("b1q", (128, HT))
    rowA, rowA_r = S("rowA", (1, 2080))
    mneg, mneg_r = S("mneg", (128, NH))
    xt, xt_r = S("xt", (128, D))
    xn1, xn1_r = S("xn1", (128, D))
    xn2, xn2_r = S("xn2", (128, D), BF16)
    PT0, PT0_r = S("PT0", (128, QH), BF16)
    PT1, PT1_r = S("PT1", (128, QH), BF16)
    scr, scr_r = S("scr", (128, D))
    mu, mu_r = S("mu", (128, 16))
    gtmp, gtmp_r = S("gtmp", (32, 128))
    gq, gq_r = S("gq", (32, 128))
    gscr, gscr_r = S("gscr", (32, 128))
    Bsb = nc.alloc_sbuf_tensor_at("Bsb", [128, QH], F32, offset=addr_of("xt"))
    mk_sb = nc.alloc_sbuf_tensor_at("mksb", [128, D], BF16,
                                    offset=addr_of("xn2"))
    htmp = nc.alloc_sbuf_tensor_at("htmp", [128, 512], F32,
                                   offset=addr_of("xn1"))
    sl2 = nc.alloc_sbuf_tensor_at("sl2", [128, 512], F32,
                                  offset=addr_of("xn1") + 2048)
    recipz = nc.alloc_sbuf_tensor_at("recipz", [128, QH], BF16,
                                     offset=addr_of("xn2"))
    Bsb_r = xt_r
    mk_sb_r = xn2_r
    htmp_r = xn1_r
    sl2_r = xn1_r
    recipz_r = xn2_r
    mred = nc.alloc_sbuf_tensor_at("mred", [128, 64], F32,
                                   offset=addr_of("xn2"))
    mred_r = xn2_r
    onesb512 = nc.alloc_sbuf_tensor_at("onesb512", [1, 512], BF16,
                                       offset=addr_of("gtmp"))
    onesK = nc.alloc_sbuf_tensor_at("onesK", [128, 128], BF16,
                                    offset=addr_of("identf"))
    mm4 = nc.alloc_sbuf_tensor_at("mm4", [128, 128], BF16,
                                  offset=addr_of("gscr"))
    mm4_r = gscr_r
    xnT2 = [nc.alloc_sbuf_tensor_at("xnT2a", [128, 2048], BF16,
                                    offset=addr_of("PT0")),
            nc.alloc_sbuf_tensor_at("xnT2b", [128, 2048], BF16,
                                    offset=addr_of("scr"))]
    xnT2R = [[PT0_r, PT1_r], [scr_r]]
    xn2x4 = nc.alloc_sbuf_tensor_at("xn2x4", [128, 4096], BF16,
                                    offset=addr_of("arnB") + 16384)
    xn2x4_r = Res("xn2x4")
    wpstg = nc.alloc_sbuf_tensor_at("wpstg", [128, 1024], F32,
                                    offset=addr_of("arnB"))
    wpsR2 = [Res("wps0"), Res("wps1")]
    xn2b = nc.alloc_sbuf_tensor_at("xn2b", [128, D], BF16,
                                   offset=addr_of("recipb"))
    xn2b_r = Res("xn2b")
    wfC = [nc.alloc_sbuf_tensor_at("wfC0", [128, 1536], F32,
                                   offset=addr_of("arnC")),
           nc.alloc_sbuf_tensor_at("wfC1", [128, 1536], F32,
                                   offset=addr_of("arnC") + 6144)]
    wfcR = [Res("wfc0"), Res("wfc1")]
    assert addr_of("PT1") == addr_of("PT0") + 2048
    assert addr_of("gq") == addr_of("gtmp") + 512
    b8 = nc.alloc_sbuf_tensor_at("b8", [128, 4096], F8, offset=addr_of("arnB"))
    xn2qT8 = nc.alloc_sbuf_tensor_at("xn2qT8", [128, 4096], F8,
                                     offset=addr_of("arnA"))
    hTbf8 = nc.alloc_sbuf_tensor_at("hTbf8", [128, HT * 512], F8,
                                    offset=addr_of("arnB"))
    w1q8 = [nc.alloc_sbuf_tensor_at("w1q8a", [128, 8192], F8,
                                    offset=addr_of("arnW") + 32768),
            nc.alloc_sbuf_tensor_at("w1q8b", [128, 8192], F8,
                                    offset=addr_of("arnW") + 40960)]
    W1R = [Res("w1q8a"), Res("w1q8b")]
    whiR = W1R
    w2bf8 = [nc.alloc_sbuf_tensor_at("w2bf8a", [128, 2048], F8,
                                     offset=addr_of("arnC") + 16384),
             nc.alloc_sbuf_tensor_at("w2bf8b", [128, 2048], F8,
                                     offset=addr_of("arnC") + 18432),
             nc.alloc_sbuf_tensor_at("w2bf8c", [128, 2048], F8,
                                     offset=addr_of("arnC") + 28672),
             nc.alloc_sbuf_tensor_at("w2bf8d", [128, 2048], F8,
                                     offset=addr_of("arnC") + 30720)]

    x2R = [Res(f"x2_{i}") for i in range(4)]
    w2bR2 = [[Res(f"w2b{i}l"), Res(f"w2b{i}h")] for i in range(4)]
    w2bR = [r for pair in w2bR2 for r in pair]
    PT = [(PT0, PT0_r), (PT1, PT1_r)]
    sumr = rowA[0:1, 0:1024]
    recip = rowA[0:1, 1024:2048]
    qqkk = rowA[0:1, 2048:2048 + 2 * NH]

    def dma(sem_name, dst_ap, src_ap, ins=(), outs=()):
        sem = em.dma_sem(sem_name)
        return em.op("sp", lambda E: E.dma_start(out=dst_ap, in_=src_ap),
                     ins=list(ins), outs=list(outs), dma=sem)

    dma("ld0", cst[:], cst_in, outs=[cst_r])
    dma("ld1", ident[:], ident_in, outs=[ident_r])
    dma("ld2", identf[:], identf_in, outs=[identf_r])
    em.op("dve", lambda E: E.memset(ones128b[:], 1.0), outs=[ones128b_r])
    em.op("dve", lambda E: E.memset(onesb1[:], 1.0), outs=[onesb1_r])
    tl_inst = mybir.InstLoadActFuncSet(
        name=nc.get_next_instruction_name(), act_func_set_id=6, ins=[], outs=[])
    em.op("act", lambda E: E.add_instruction(tl_inst))

    def load_cols(dst_ap, dst_r, src, ncols, quant):
        dma("ldg", gtmp[0:ncols, :], src, outs=[gtmp_r])
        sb_, sr_ = gtmp, gtmp_r
        if quant:
            emit_quant(em, "ANTQ_E4M3", gq[0:ncols, :], gtmp[0:ncols, :],
                       [gtmp_r], [gq_r], gscr[0:ncols, :], gscr_r)
            sb_, sr_ = gq, gq_r
        em.op("pe", lambda E: E.transpose(pp[0:128, 3072:3072 + ncols],
                                          sb_[0:ncols, :],
                                          identf[0:ncols, 0:ncols]),
              ins=[sr_, identf_r], outs=pb(3072, 3584))
        em.op("dve", lambda E: E.tensor_copy(dst_ap, pp[0:128, 3072:3072 + ncols]),
              ins=pb(3072, 3584), outs=[dst_r])

    load_cols(gam[:, 0:DC], gam_r, gamma_in, DC, False)
    load_cols(bet[:, 0:DC], bet_r, beta_in, DC, False)
    load_cols(b1q[:, 0:HT], b1q_r, b1_in, HT, True)
    em.op("dve", lambda E: E.memset(onesb512[:], 1.0),
          outs=[gtmp_r, gq_r])
    em.op("dve", lambda E: E.memset(onesK[:], 1.0), outs=[identf_r])

    wfR = [Res("wf0"), Res("wf1")]
    wfh = [FA[:, 0:1536], FA[:, 1536:3072]]

    def emit_wq_stream(b):
        if b == 0:
            bufs, bR, extra = wfh, wfR, [FA_r]
        else:
            bufs, bR, extra = [wfC[0][:], wfC[1][:]], wfcR, [C_r]
        for c in range(DC):
            for hf in range(2):
                em.op("sp", lambda E, c=c, hf=hf, bufs=bufs: E.dma_start(
                    out=bufs[hf],
                    in_=Wqkv[c * 128:(c + 1) * 128,
                             hf * 1536:(hf + 1) * 1536]),
                    outs=[bR[hf]] + extra,
                    dma=em.dma_sem(f"wqs{(2 * c + hf) % 4}"))
                if b == 0:
                    for nb_ in range(3):
                        lo = hf * 1536 + nb_ * 512
                        em.op("pe", lambda E, c=c, hf=hf, nb_=nb_, lo=lo,
                              bufs=bufs: E.matmul(
                                  pp[0:1, lo:lo + 512],
                                  bet[:, c:c + 1],
                                  bufs[hf][:, nb_ * 512:(nb_ + 1) * 512],
                                  start=(c == 0), stop=(c == DC - 1)),
                              ins=[bet_r, bR[hf]], outs=pb(lo, lo + 512))
                em.op("dve", lambda E, c=c, hf=hf, bufs=bufs: E.tensor_scalar(
                    wq(c)[:, hf * 1536:(hf + 1) * 1536], bufs[hf],
                    gam[:, c:c + 1], None, ALU.mult),
                    ins=[bR[hf], gam_r] + extra, outs=wq_res(c))
        if b == 0:
            dma("ldbq", scr[0:96:32, 0:1024],
                bqkv_in.rearrange("a (b c) -> (a b) c", c=1024),
                outs=[scr_r])
            for j in range(3):
                em.op("dve", lambda E, j=j: E.tensor_tensor(
                    bcombb[0:1, j * 1024:(j + 1) * 1024],
                    pp[0:1, j * 1024:(j + 1) * 1024],
                    scr[32 * j:32 * j + 1, 0:1024], ALU.add),
                    ins=pb(j * 1024, (j + 1) * 1024) + [scr_r],
                    outs=[bcombb_r])

    emit_wq_stream(0)

    p0f = nc.alloc_sbuf_tensor_at("p0f", [128, 4096], F32,
                                  offset=addr_of("arnC"))
    p0scr = nc.alloc_sbuf_tensor_at("p0scr", [128, 1024], F32,
                                    offset=addr_of("xn1"))
    p0R = [Res(f"p0_{i}") for i in range(4)]
    b8R = [Res(f"b8_{i}") for i in range(4)]
    spillR = Res("p0spill")
    spillR1 = Res("p0spillw1")

    def emit_p0():
        for c in range(DC):
            for piece in range(2):
                sl = piece % 2
                em.op("sp", lambda E, c=c, piece=piece, sl=sl: E.dma_start(
                    out=p0f[:, sl * 2048:sl * 2048 + 2048],
                    in_=w1T[c * 128:(c + 1) * 128,
                            piece * 2048:(piece + 1) * 2048]),
                    outs=[p0R[sl]],
                    dma=em.dma_sem(f"p0a{(2 * c + piece) % 4}"))
                for qq in range(2):
                    emit_quant(em, "ANTQ_E4M3",
                               b8[:, (2 * piece + qq) * 1024:
                                  (2 * piece + qq + 1) * 1024],
                               p0f[:, (sl * 2 + qq) * 1024:
                                   (sl * 2 + qq + 1) * 1024],
                               [p0R[sl]], [b8R[sl]], p0scr[:], xn1_r)
                em.op("sp", lambda E, c=c, piece=piece, sl=sl: E.dma_start(
                    out=w1q_sc[c * 128:(c + 1) * 128,
                               piece * 2048:(piece + 1) * 2048],
                    in_=b8[:, piece * 2048:(piece + 1) * 2048]),
                    ins=[b8R[sl]], outs=[spillR1],
                    dma=em.dma_sem(f"p0o{sl}"))
        for c in range(HT):
            sl = c % 4
            em.op("sp", lambda E, c=c, sl=sl: E.dma_start(
                out=p0f[:, sl * 1024:sl * 1024 + 1024],
                in_=w2T[c * 128:(c + 1) * 128, :]),
                outs=[p0R[sl]], dma=em.dma_sem(f"p0c{c % 4}"))
            emit_quant(em, "ANTQ_E4M3", b8[:, sl * 1024:sl * 1024 + 1024],
                       p0f[:, sl * 1024:sl * 1024 + 1024],
                       [p0R[sl]], [b8R[sl]], p0scr[:], xn1_r)
            em.op("sp", lambda E, c=c, sl=sl: E.dma_start(
                out=w2q_sc[c * 128:(c + 1) * 128, :],
                in_=b8[:, sl * 1024:sl * 1024 + 1024]),
                ins=[b8R[sl]], outs=[spillR],
                dma=em.dma_sem(f"p0d{sl % 2}"))

    for b in range(NB):
        if b > 0:
            emit_wq_stream(b)

        def vb_memset():
            em.op("dve", lambda E: E.memset(Vb, 1.0),
                  outs=[C_r, B_r, xn2x4_r] + w2bR + p0R)

        def emit_ln(src_ap, src_ins, out_ap, out_r, dump_ap, dump_r, ms):
            c0 = 8 * ms
            em.op("act", lambda E: E.activation(
                dump_ap, src_ap, AF.Copy, bias=0.0, scale=1.0,
                accum_out=mu[:, c0:c0 + 1]),
                ins=list(src_ins), outs=[dump_r, mu_r])
            em.op("act", lambda E: E.activation(
                dump_ap, src_ap, AF.Square, bias=0.0, scale=1.0,
                accum_out=mu[:, c0 + 1:c0 + 2]),
                ins=list(src_ins), outs=[dump_r, mu_r])
            em.op("dve", lambda E: E.tensor_scalar(
                mu[:, c0 + 2:c0 + 3], mu[:, c0:c0 + 1], 1.0 / D, None,
                ALU.mult), ins=[mu_r], outs=[mu_r])
            em.op("dve", lambda E: E.tensor_scalar(
                mu[:, c0 + 3:c0 + 4], mu[:, c0 + 1:c0 + 2], 1.0 / D, None,
                ALU.mult), ins=[mu_r], outs=[mu_r])
            em.op("dve", lambda E: E.tensor_tensor(
                mu[:, c0 + 4:c0 + 5], mu[:, c0 + 2:c0 + 3],
                mu[:, c0 + 2:c0 + 3], ALU.mult),
                ins=[mu_r], outs=[mu_r])
            em.op("dve", lambda E: E.tensor_tensor(
                mu[:, c0 + 5:c0 + 6], mu[:, c0 + 3:c0 + 4],
                mu[:, c0 + 4:c0 + 5], ALU.subtract),
                ins=[mu_r], outs=[mu_r])
            em.op("act", lambda E: E.activation(
                mu[:, c0 + 6:c0 + 7], mu[:, c0 + 5:c0 + 6], AF.Ln,
                bias=cst[:, C_EPS:C_EPS + 1], scale=1.0),
                ins=[mu_r, cst_r], outs=[mu_r])
            em.op("act", lambda E: E.activation(
                mu[:, c0 + 7:c0 + 8], mu[:, c0 + 6:c0 + 7], AF.Exp, bias=0.0,
                scale=cst[:, C_NEGHALF:C_NEGHALF + 1]),
                ins=[mu_r, cst_r], outs=[mu_r])
            em.op("dve", lambda E: E.tensor_scalar(
                out_ap, src_ap, mu[:, c0 + 2:c0 + 3], mu[:, c0 + 7:c0 + 8],
                ALU.subtract, ALU.mult),
                ins=list(src_ins) + [mu_r], outs=[out_r])

        def p1_tiles_ln(g):
            for tl in range(2):
                ts = 2 * g + tl
                sl = x2sl(ts)
                slr = x2R[ts % 4]
                xdeps = [slr, FA_r, wfR[0], wfR[1]] if b == 0 else [slr]
                dma(f"xin{ts % 4}", sl, x_in[b, ts * 128:(ts + 1) * 128, :],
                    outs=xdeps)
                ms = ts % 2
                emit_ln(sl, [slr], xn1[:], xn1_r, xn1[:], xn1_r, ms)
                dst = xn2 if tl == 0 else xn2b
                dstr = xn2_r if tl == 0 else xn2b_r
                emit_ln(xn1[:], [xn1_r], dst[:], dstr, dst[:], dstr, ms)

        def p1_tiles_tr(g):
            buf = g % 2
            for tl in range(2):
                src = xn2 if tl == 0 else xn2b
                srcr = xn2_r if tl == 0 else xn2b_r
                for c in range(DC):
                    em.op("pe", lambda E, c=c, src=src: E.transpose(
                        ppb[:, c * 128:(c + 1) * 128],
                        src[:, c * 128:(c + 1) * 128], ident[:]),
                        ins=[srcr, ident_r], outs=[ppb_r])
                xview = xnT2[buf].rearrange("p (c t) -> p c t", t=256)[
                    :, :, tl * 128:(tl + 1) * 128]
                em.op("dve", lambda E, xview=xview: E.tensor_copy(
                    xview,
                    ppb[:, 0:1024].rearrange("p (c t) -> p c t", t=128)),
                    ins=[ppb_r], outs=list(xnT2R[buf]))

        def p1_group_mms(g):
            buf = g % 2
            xb = xnT2[buf]
            xbR = list(xnT2R[buf])
            for j in range(16):
                off = j * 128 if j < 8 else 1024 + (j - 8) * 128
                bank = (j % 4) * 512
                for c in range(DC):
                    em.op("pe", lambda E, c=c, off=off, bank=bank: E.matmul(
                        pp[:, bank:bank + 256],
                        wq(c)[:, off:off + 128],
                        xb[:, c * 256:(c + 1) * 256],
                        start=(c == 0), stop=False),
                        ins=xbR + wq_res(c), outs=pb(bank, bank + 256))
                em.op("pe", lambda E, off=off, bank=bank: E.matmul(
                    pp[:, bank:bank + 256],
                    bcombb[0:1, off:off + 128], onesb512[0:1, 0:256],
                    start=False, stop=True),
                    ins=[bcombb_r, gtmp_r, gq_r], outs=pb(bank, bank + 256))
                dst, dstr = (QTb, A_r) if j < 8 else (KTb, B_r)
                col = (j if j < 8 else j - 8) * T + g * 256
                evac_ins = pb(bank, bank + 256) + (
                    [spillR, spillR1] if j == 8 else [])
                em.op("act", lambda E, dst=dst, col=col, bank=bank:
                      E.activation(dst[:, col:col + 256],
                                   pp[:, bank:bank + 256],
                                   AF.Copy, bias=0.0, scale=1.0),
                      ins=evac_ins, outs=[dstr])
                em.op("dve", lambda E, dst=dst, col=col, g=g, j=j:
                      E.tensor_reduce(mm4[:, g * 16 + j:g * 16 + j + 1],
                                      dst[:, col:col + 256], AX.X, ALU.max,
                                      apply_absolute_value=True),
                      ins=[dstr], outs=[mm4_r])
            for tl in range(2):
                ts = 2 * g + tl
                for c in range(DC):
                    for nb_ in range(2):
                        em.op("pe", lambda E, c=c, tl=tl, nb_=nb_: E.matmul(
                            pp[:, 2048 + nb_ * 512:2048 + (nb_ + 1) * 512],
                            xb[:, c * 256 + tl * 128:c * 256 + tl * 128 + 128],
                            wq(c)[:, 2048 + nb_ * 512:2048 + (nb_ + 1) * 512],
                            start=(c == 0), stop=False),
                            ins=xbR + wq_res(c),
                            outs=pb(2048 + nb_ * 512, 2048 + (nb_ + 1) * 512))
                for nb_ in range(2):
                    em.op("pe", lambda E, nb_=nb_: E.matmul(
                        pp[:, 2048 + nb_ * 512:2048 + (nb_ + 1) * 512],
                        onesb1[0:1, 0:128],
                        bcombb[0:1, 2048 + nb_ * 512:2048 + (nb_ + 1) * 512],
                        start=False, stop=True),
                        ins=[onesb1_r, bcombb_r],
                        outs=pb(2048 + nb_ * 512, 2048 + (nb_ + 1) * 512))
                vdst = Vb[:, ts * NH * 65:(ts + 1) * NH * 65].rearrange(
                    "p (h c) -> p h c", c=65)[:, :, 0:64]
                vsrc = pp[:, 2048:3072].rearrange("p (h c) -> p h c", c=64)
                em.op("dve", lambda E, vdst=vdst, vsrc=vsrc: E.tensor_copy(
                    vdst, vsrc), ins=pb(2048, 3072), outs=[C_r])

        if b == 0:
            p1_tiles_ln(0)
            p1_tiles_tr(0)
            p1_tiles_ln(1)
            emit_p0()
            vb_memset()
            p1_tiles_tr(1)
            pre = 2
        else:
            vb_memset()
            p1_tiles_ln(0)
            p1_tiles_tr(0)
            pre = 1
        for g in range(TT // 2):
            if pre <= g + 1 < TT // 2:
                p1_tiles_ln(g + 1)
            p1_group_mms(g)
            if pre <= g + 1 < TT // 2:
                p1_tiles_tr(g + 1)

        mmv = mm4.rearrange("p (g j) -> p j g", j=16)
        em.op("dve", lambda E: E.tensor_reduce(
            mred[:, 0:16], mmv, AX.X, ALU.max),
            ins=[mm4_r], outs=[mred_r])
        em.op("dve", lambda E: E.tensor_tensor(
            mred[:, 16:32], mred[:, 0:16], mred[:, 0:16], ALU.mult),
            ins=[mred_r], outs=[mred_r])
        em.op("pool", lambda E: E.tensor_reduce(
            mred[0:1, 32:48], mred[0:64, 16:32], AX.C, ALU.add),
            ins=[mred_r], outs=[mred_r])
        em.op("pool", lambda E: E.tensor_reduce(
            mred[0:1, 48:64], mred[64:128, 16:32], AX.C, ALU.add),
            ins=[mred_r], outs=[mred_r])
        qkview = rowA[0:1, 2048:2080].rearrange("a (x h two) -> a x two h",
                                                two=2, h=8)
        for i2, c0 in enumerate((32, 48, 40, 56)):
            em.op("dve", lambda E, i2=i2, c0=c0: E.tensor_copy(
                qkview[:, i2 // 2, i2 % 2, :],
                mred[0:1, c0:c0 + 8]),
                ins=[mred_r], outs=[rowA_r])
        em.op("dve", lambda E: E.tensor_scalar(qqkk, qqkk, 1e-18, None,
                                               ALU.max),
              ins=[rowA_r], outs=[rowA_r])
        em.op("act", lambda E: E.activation(qqkk, qqkk, AF.Ln,
                                            bias=0.0, scale=1.0),
              ins=[rowA_r], outs=[rowA_r])
        em.op("dve", lambda E: E.tensor_tensor(
            qqkk[0:1, 0:NH], qqkk[0:1, 0:NH], qqkk[0:1, NH:2 * NH], ALU.add),
            ins=[rowA_r], outs=[rowA_r])
        em.op("act", lambda E: E.activation(
            qqkk[0:1, NH:2 * NH], qqkk[0:1, 0:NH], AF.Exp,
            bias=cst[0:1, C_NEGLN8:C_NEGLN8 + 1],
            scale=cst[0:1, C_HALF:C_HALF + 1]),
            ins=[rowA_r, cst_r], outs=[rowA_r])
        em.op("dve", lambda E: E.tensor_scalar(
            qqkkb[:], qqkk[0:1, NH:2 * NH], -1.0, 30.0, ALU.mult, ALU.add),
            ins=[rowA_r], outs=[qqkkb_r])
        em.op("pe", lambda E: E.matmul(
            pp[0:128, 3072:3072 + NH], onesb1[0:1, 0:128],
            qqkkb[:], start=True, stop=True),
            ins=[onesb1_r, qqkkb_r], outs=pb(3072, 3584))
        em.op("dve", lambda E: E.tensor_copy(mneg[:], pp[0:128, 3072:3072 + NH]),
              ins=pb(3072, 3584), outs=[mneg_r])

        em.op("dve", lambda E: E.memset(recipz[64:128, :], 0.0),
              outs=[recipz_r])
        sums_slots = [(Bsb[64:65, 512:1024], xt_r), (sl2[64:65, 0:512],
                                                     sl2_r)]

        def emit_scores(h, qq, kt):
            p, s2 = h // 2, h % 2
            base = 64 * s2
            q0 = qq * 512
            bank = (kt % 4) * 512
            em.op("pe", lambda E, kt=kt, bank=bank, p=p, base=base, q0=q0:
                  E.matmul(
                      pp[:, bank:bank + 512],
                      KTb[base:base + 64,
                          p * T + kt * 128:p * T + (kt + 1) * 128],
                      QTb[base:base + 64, p * T + q0:p * T + q0 + 512],
                      start=True, stop=True),
                  ins=[B_r, A_r], outs=pb(bank, bank + 512))

        def emit_exp_pair(h, st):
            lo = (st % 2) * 1024
            ptv, ptr = PT[st % 2]
            em.op("act", lambda E, lo=lo, ptv=ptv, h=h: E.activation(
                ptv[:], pp[:, lo:lo + 1024], AF.Exp,
                bias=mneg[:, h:h + 1],
                scale=cst[:, C_SCALE8:C_SCALE8 + 1]),
                ins=pb(lo, lo + 1024) + [mneg_r, cst_r],
                outs=[ptr])

        def emit_ctx(h, kt):
            ptv, ptr = PT[(kt // 2) % 2]
            pcol = (kt % 2) * 512
            vc = kt * NH * 65 + h * 65
            for half in range(2):
                hb = 64 * half
                em.op("pe", lambda E, kt=kt, hb=hb, vc=vc, ptv=ptv,
                      pcol=pcol, half=half: E.matmul(
                          pp[0:65, 2048 + half * 512:2560 + half * 512],
                          Vb[hb:hb + 64, vc:vc + 65],
                          ptv[hb:hb + 64, pcol:pcol + 512],
                          start=(kt == 0), stop=(kt == TT - 1)),
                      ins=[C_r, ptr],
                      outs=pb(2048 + half * 512, 2560 + half * 512))

        def emit_evac(h, qq, u):
            p, s2 = h // 2, h % 2
            base = 64 * s2
            q0 = qq * 512
            srow, srow_r = sums_slots[u % 2]
            em.op("dve", lambda E: E.tensor_copy(
                htmp[0:65, :], pp[0:65, 2560:3072]),
                ins=pb(2560, 3072), outs=[htmp_r])
            em.op("dve", lambda E, p=p, base=base, q0=q0: E.tensor_tensor(
                ctx_all[base:base + 64, p * T + q0:p * T + q0 + 512],
                pp[0:64, 2048:2560], htmp[0:64, :], ALU.add),
                ins=pb(2048, 2560) + [htmp_r], outs=[W_r])
            em.op("dve", lambda E, srow=srow: E.tensor_tensor(
                srow, pp[64:65, 2048:2560], htmp[64:65, :], ALU.add),
                ins=pb(2048, 2560) + [htmp_r], outs=[srow_r])

        def emit_tail(h, qq, u):
            p, s2 = h // 2, h % 2
            base = 64 * s2
            q0 = qq * 512
            srow, srow_r = sums_slots[u % 2]
            em.op("dve", lambda E, srow=srow: E.tensor_scalar(
                srow, srow, 1e-12, None, ALU.max),
                ins=[srow_r], outs=[srow_r])
            em.op("act", lambda E, srow=srow: E.activation(
                srow, srow, AF.Ln, bias=0.0, scale=1.0),
                ins=[srow_r], outs=[srow_r])
            em.op("act", lambda E, srow=srow: E.activation(
                recipz[64:65, 0:512], srow, AF.Exp, bias=0.0,
                scale=cst[0:1, C_NEG1:C_NEG1 + 1]),
                ins=[srow_r, cst_r], outs=[recipz_r])
            em.op("pe", lambda E: E.matmul(
                pp[0:128, 3072:3584], onesK[64:128, :],
                recipz[64:128, 0:512], start=True, stop=True),
                ins=[identf_r, recipz_r], outs=pb(3072, 3584))
            em.op("dve", lambda E: E.tensor_copy(
                Bsb[:, 0:512], pp[0:128, 3072:3584]),
                ins=pb(3072, 3584), outs=[Bsb_r])
            em.op("dve", lambda E, p=p, base=base, q0=q0: E.tensor_tensor(
                ctx_all[base:base + 64, p * T + q0:p * T + q0 + 512],
                ctx_all[base:base + 64, p * T + q0:p * T + q0 + 512],
                Bsb[base:base + 64, 0:512], ALU.mult),
                ins=[Bsb_r, W_r], outs=[W_r])

        def emit_ctx_half(h, kt, half):
            ptv, ptr = PT[(kt // 2) % 2]
            pcol = (kt % 2) * 512
            vc = kt * NH * 65 + h * 65
            hb = 64 * half
            em.op("pe", lambda E, kt=kt, hb=hb, vc=vc, ptv=ptv,
                  pcol=pcol, half=half: E.matmul(
                      pp[0:65, 2048 + half * 512:2560 + half * 512],
                      Vb[hb:hb + 64, vc:vc + 65],
                      ptv[hb:hb + 64, pcol:pcol + 512],
                      start=(kt == 0), stop=(kt == TT - 1)),
                  ins=[C_r, ptr],
                  outs=pb(2048 + half * 512, 2560 + half * 512))

        units = [(h, qq) for h in range(NH) for qq in range(4)]
        pend = None
        for u, (h, qq) in enumerate(units):
            sc_par = h % 2
            for st in range(TT // 2):
                if st >= 2:
                    k0, k1 = 2 * st - 4, 2 * st - 3
                    if sc_par == 0:
                        emit_ctx_half(h, k0, 1)
                        emit_scores(h, qq, 2 * st)
                        emit_ctx_half(h, k1, 1)
                        emit_scores(h, qq, 2 * st + 1)
                        emit_ctx_half(h, k0, 0)
                        emit_ctx_half(h, k1, 0)
                    else:
                        emit_ctx_half(h, k0, 0)
                        emit_scores(h, qq, 2 * st)
                        emit_ctx_half(h, k1, 0)
                        emit_scores(h, qq, 2 * st + 1)
                        emit_ctx_half(h, k0, 1)
                        emit_ctx_half(h, k1, 1)
                else:
                    emit_scores(h, qq, 2 * st)
                    emit_scores(h, qq, 2 * st + 1)
                if st == 3 and pend is not None:
                    emit_tail(*pend)
                    pend = None
                emit_exp_pair(h, st)
            for k2 in (TT - 4, TT - 3, TT - 2, TT - 1):
                emit_ctx(h, k2)
            emit_evac(h, qq, u)
            pend = (h, qq, u)
        emit_tail(*pend)

        for c in range(DC):
            for hh in range(2):
                first = (c == 0 and hh == 0)
                em.op("sp", lambda E, c=c, hh=hh: E.dma_start(
                    out=wpstg[:, hh * 512:(hh + 1) * 512],
                    in_=Wproj[c * 128:(c + 1) * 128,
                              hh * 512:(hh + 1) * 512]),
                    outs=[wpsR2[hh]] + ([B_r] if first else []),
                    dma=em.dma_sem(f"wps{hh}"))
                em.op("dve", lambda E, c=c, hh=hh: E.tensor_copy(
                    wp(c)[:, hh * 512:(hh + 1) * 512],
                    wpstg[:, hh * 512:(hh + 1) * 512]),
                    ins=[wpsR2[hh], B_r], outs=[C_r])
        dma("ldbp", bproj_sb[:], bproj_in, ins=[C_r], outs=[bproj_r])
        em.op("dve", lambda E: E.tensor_copy(bprojb[:], bproj_sb[:]),
              ins=[bproj_r], outs=[bprojb_r])
        dma("ldb2", xn1[0:1, 0:D], b2_in, outs=[xn1_r])
        emit_quant(em, "ANTQ_E4M3", b2q[:], xn1[0:1, 0:D],
                   [xn1_r, C_r], [b2q_r], scr[0:1, 0:D], scr_r)
        em.op("dve", lambda E: E.tensor_copy(b2qb[:], b2q[:]),
              ins=[b2q_r], outs=[b2qb_r])

        for tc in range(4):
            t0 = tc * 512
            for tl in range(4):
                ts = tc * 4 + tl
                po = 0
                for c in range(DC):
                    for nb_ in range(2):
                        em.op("pe", lambda E, c=c, nb_=nb_, ts=ts, po=po:
                              E.matmul(
                                  pp[:, po + nb_ * 512:po + (nb_ + 1) * 512],
                                  ctx_all[:, c * T + ts * 128:
                                          c * T + (ts + 1) * 128],
                                  wp(c)[:, nb_ * 512:(nb_ + 1) * 512],
                                  start=(c == 0), stop=False),
                              ins=[W_r, C_r],
                              outs=pb(po + nb_ * 512, po + (nb_ + 1) * 512))
                for nb_ in range(2):
                    em.op("pe", lambda E, nb_=nb_, po=po: E.matmul(
                        pp[:, po + nb_ * 512:po + (nb_ + 1) * 512],
                        onesb1[0:1, 0:128],
                        bprojb[0:1, nb_ * 512:(nb_ + 1) * 512],
                        start=False, stop=True),
                        ins=[onesb1_r, bprojb_r],
                        outs=pb(po + nb_ * 512, po + (nb_ + 1) * 512))
                dma("xinp", xt[:], x_in[b, ts * 128:(ts + 1) * 128, :],
                    outs=[xt_r])
                em.op("dve", lambda E, ts=ts, po=po: E.tensor_tensor(
                    x2sl(ts), pp[:, po:po + D], xt[:], ALU.add),
                    ins=pb(po, po + D) + [xt_r, FA_r], outs=[x2R[ts % 4]])
                emit_ln(x2sl(ts), [x2R[ts % 4], FA_r], xn1[:], xn1_r,
                        xn1[:], xn1_r, ts % 2)
                emit_quant(em, "ANTQ_E4M3",
                           xn2x4[:, tl * 1024:(tl + 1) * 1024], xn1[:],
                           [xn1_r], [xn2x4_r], scr[:], scr_r)
            for tl in range(4):
                for c in range(DC):
                    em.op("pe", lambda E, c=c, tl=tl: E.transpose(
                        ppb[:, c * 128:(c + 1) * 128],
                        xn2x4[:, tl * 1024 + c * 128:tl * 1024 + (c + 1) * 128],
                        ident[:]),
                        ins=[xn2x4_r, ident_r], outs=[ppb_r])
                xview = xn2qT8.rearrange("p (r c) -> p r c", c=512)[
                    :, :, tl * 128:(tl + 1) * 128]
                em.op("dve", lambda E, xview=xview: E.tensor_copy(
                    xview, ppb[:, 0:1024].rearrange("p (r c) -> p r c", c=128)),
                    ins=[ppb_r], outs=[A_r])

            def w1_stream(quarter):
                slq = quarter % 2
                for c in range(DC):
                    em.op("sp", lambda E, c=c, quarter=quarter, slq=slq:
                          E.dma_start(
                              out=w1q8[slq][:, c * 1024:(c + 1) * 1024],
                              in_=w1q_sc[c * 128:(c + 1) * 128,
                                         quarter * 1024:(quarter + 1) * 1024]),
                          ins=[spillR1],
                          outs=[W1R[slq]],
                          dma=em.dma_sem(f"w1s{(2 * slq + c % 2)}"))

            if tc == 0:
                w1_stream(0)
            for quarter in range(4):
                if quarter + 1 < 4:
                    w1_stream(quarter + 1)
                elif tc + 1 < 4:
                    w1_stream(0)
                slq = quarter % 2
                w1v = w1q8[slq].rearrange("p (c q) -> p c q", q=1024)
                x8v = xn2qT8.rearrange("p (c t) -> p c t", t=512)
                for j in range(8):
                    Ht = quarter * 8 + j
                    hb = 2048 + 512 * (j % 2)
                    for c2 in range(DC // 2):
                        em.op("pe", lambda E, c2=c2, j=j, hb=hb, w1v=w1v,
                              x8v=x8v: E.matmul(
                                  pp[:, hb:hb + 512],
                                  w1v[:, 2 * c2:2 * c2 + 2,
                                      j * 128:(j + 1) * 128],
                                  x8v[:, 2 * c2:2 * c2 + 2, :],
                                  start=(c2 == 0), stop=(c2 == DC // 2 - 1),
                                  perf_mode=DR),
                              ins=[W1R[slq], A_r], outs=pb(hb, hb + 512))
                    em.op("act", lambda E, hb=hb, Ht=Ht: E.activation(
                        xn1[:, 0:512], pp[:, hb:hb + 512], AF.Relu,
                        bias=b1q[:, Ht:Ht + 1], scale=1.0),
                        ins=pb(hb, hb + 512) + [b1q_r], outs=[xn1_r])
                    emit_quant(em, "ANTQ_FP16", yfull[:, 0:512],
                               xn1[:, 0:512], [xn1_r], [yfull_r],
                               scr[:, 0:512], scr_r)
                    emit_quant(em, "ANTQ_E4M3",
                               hTbf8[:, Ht * 512:(Ht + 1) * 512],
                               yfull[:, 0:512], [yfull_r, A_r], [B_r],
                               scr[:, 512:1024], scr_r)

            hTv = hTbf8.rearrange("p (H t) -> p H t", t=512)
            for tg in range(2):
                for Hp in range(HT // 2):
                    wbv = w2bf8[Hp % 4]
                    wbrL = w2bR2[Hp % 4]
                    wbj = wbv[:].rearrange("p (j m) -> p j m", j=2)
                    for hh in range(2):
                        em.op("sp", lambda E, Hp=Hp, wbj=wbj, hh=hh:
                              E.dma_start(
                                  out=wbj[:, :, hh * 512:(hh + 1) * 512],
                                  in_=w2q_sc[2 * Hp * 128:(2 * Hp + 2) * 128,
                                             hh * 512:(hh + 1) * 512]
                                  .rearrange("(j p) m -> p j m", j=2)),
                              ins=[C_r, spillR], outs=[wbrL[hh]],
                              dma=em.dma_sem(f"w2s{(Hp % 4) * 2 + hh}"))
                    for tl2 in range(2):
                        tl = tg * 2 + tl2
                        yb = 1024 + tl2 * 1024
                        for nb_ in range(2):
                            em.op("pe", lambda E, Hp=Hp, tl=tl, yb=yb,
                                  nb_=nb_, wbj=wbj: E.matmul(
                                      pp[:, yb + nb_ * 512:
                                         yb + (nb_ + 1) * 512],
                                      hTv[:, 2 * Hp:2 * Hp + 2,
                                          tl * 128:(tl + 1) * 128],
                                      wbj[:, :, nb_ * 512:(nb_ + 1) * 512],
                                      start=(Hp == 0), stop=False,
                                      perf_mode=DR),
                                  ins=[B_r, wbrL[nb_]],
                                  outs=pb(yb + nb_ * 512,
                                          yb + (nb_ + 1) * 512))
                for tl2 in range(2):
                    tl = tg * 2 + tl2
                    ts = tc * 4 + tl
                    yb = 1024 + tl2 * 1024
                    for nb_ in range(2):
                        em.op("pe", lambda E, yb=yb, nb_=nb_: E.matmul(
                            pp[:, yb + nb_ * 512:yb + (nb_ + 1) * 512],
                            onesb1[0:1, 0:128],
                            b2qb[0:1, nb_ * 512:(nb_ + 1) * 512],
                            start=False, stop=True),
                            ins=[onesb1_r, b2qb_r],
                            outs=pb(yb + nb_ * 512, yb + (nb_ + 1) * 512))
                    dma("mkin", mk_sb[:], mask_in[b, ts * 128:(ts + 1) * 128, :],
                        outs=[mk_sb_r])
                    emit_quant(em, "ANTQ_FP16", yq[:], pp[:, yb:yb + 1024],
                               pb(yb, yb + 1024) + [A_r], [yq_r],
                               scr[:], scr_r)
                    em.op("dve", lambda E: E.tensor_tensor(
                        yq[:], yq[:], mk_sb[:], ALU.mult),
                        ins=[yq_r, mk_sb_r], outs=[yq_r])
                    em.op("dve", lambda E, ts=ts: E.tensor_tensor(
                        x2sl(ts), x2sl(ts), yq[:], ALU.add),
                        ins=[x2R[ts % 4], yq_r, FA_r], outs=[x2R[ts % 4]])
                    em.op("sp", lambda E, ts=ts: E.dma_start(
                        out=out_hbm[b, ts * 128:(ts + 1) * 128, :],
                        in_=x2sl(ts)),
                        ins=[x2R[ts % 4], FA_r],
                        dma=em.dma_sem(f"oout{ts % 4}"))

    for s in ("oout0", "oout1", "oout2", "oout3"):
        if s in em.cnt:
            em.wait("sp", s, em.cnt[s])
    em.flush()
    em.close()


_PROG_CACHE = {}


def _get_program():
    if "nc" not in _PROG_CACHE:
        nc = bacc.Bacc("TRN2", target_bir_lowering=False, debug=False)
        build_program(nc)
        nc.finalize()
        _PROG_CACHE["nc"] = nc
    return _PROG_CACHE["nc"]


def _rotl32(x, d):
    return ((x << np.uint32(d)) | (x >> np.uint32(32 - d))).astype(np.uint32)


def _threefry2x32(k0, k1, x0, x1):
    x0 = x0.astype(np.uint32).copy()
    x1 = x1.astype(np.uint32).copy()
    ks0 = np.uint32(k0)
    ks1 = np.uint32(k1)
    ks2 = np.uint32(ks0 ^ ks1 ^ np.uint32(0x1BD11BDA))
    rot = [(13, 15, 26, 6), (17, 29, 16, 24)]
    x0 = (x0 + ks0).astype(np.uint32)
    x1 = (x1 + ks1).astype(np.uint32)
    ks = [ks0, ks1, ks2]
    for i in range(5):
        for d in rot[i % 2]:
            x0 = (x0 + x1).astype(np.uint32)
            x1 = _rotl32(x1, d)
            x1 = (x1 ^ x0).astype(np.uint32)
        x0 = (x0 + ks[(i + 1) % 3]).astype(np.uint32)
        x1 = (x1 + ks[(i + 2) % 3] + np.uint32(i + 1)).astype(np.uint32)
    return x0, x1


def _dropout_keep_mask(n):
    i64 = np.arange(n, dtype=np.uint64)
    hi = (i64 >> np.uint64(32)).astype(np.uint32)
    lo = (i64 & np.uint64(0xFFFFFFFF)).astype(np.uint32)
    a, b = _threefry2x32(np.uint32(0), np.uint32(42), hi, lo)
    if os.environ.get("BASS_SIM_FP64") == "1":
        bits64 = (a.astype(np.uint64) << np.uint64(32)) | b.astype(np.uint64)
        fb = (bits64 >> np.uint64(12)) | np.uint64(0x3FF0000000000000)
        u = fb.view(np.float64) - np.float64(1.0)
        return u < np.float64(0.8)
    bits = a ^ b
    fb = (bits >> np.uint32(9)) | np.uint32(0x3F800000)
    u = fb.view(np.float32) - np.float32(1.0)
    return u < np.float32(0.8)


def kernel(**inputs):
    import ml_dtypes

    x = np.ascontiguousarray(np.asarray(inputs["L0_attn_x"], dtype=np.float32))
    gamma = np.asarray(inputs["L0_attn_ln_gamma"], dtype=np.float32)
    beta = np.asarray(inputs["L0_attn_ln_beta"], dtype=np.float32)
    Wq = np.ascontiguousarray(np.asarray(inputs["L0_attn_W_qkv"], np.float32))
    bq = np.asarray(inputs["L0_attn_b_qkv"], np.float32).reshape(1, 3 * D)
    Wp = np.ascontiguousarray(np.asarray(inputs["L0_attn_W_proj"], np.float32))
    bp = np.asarray(inputs["L0_attn_b_proj"], np.float32).reshape(1, D)
    w1 = np.asarray(inputs["L0_ffn_w1"], np.float32)
    b1 = np.asarray(inputs["L0_ffn_b1"], np.float32).reshape(HT, 128)
    w2 = np.asarray(inputs["L0_ffn_w2"], np.float32)
    b2 = np.asarray(inputs["L0_ffn_b2"], np.float32).reshape(1, D)

    w1t = np.ascontiguousarray(w1.T)
    w2t = np.ascontiguousarray(w2.T)
    gam = np.ascontiguousarray(gamma.reshape(DC, 128))
    bet = np.ascontiguousarray(beta.reshape(DC, 128))
    idn = np.eye(128, dtype=np.float32).astype(ml_dtypes.bfloat16)
    idf = np.eye(128, dtype=np.float32)
    cstv = _make_consts()

    keep = _dropout_keep_mask(NCORES * NB * T * D).reshape(NCORES * NB, T, D)
    mk = np.where(keep, np.float32(1.25), np.float32(0.0)).astype(
        ml_dtypes.bfloat16)

    nc = _get_program()
    shared = dict(wqkv=Wq, bqkv=bq, wproj=Wp, bproj=bp, w1t=w1t, b1=b1,
                  w2t=w2t, b2=b2, gam=gam, bet=bet, idn=idn, idf=idf,
                  cst=cstv)
    in_maps = []
    for i in range(NCORES):
        m = dict(shared)
        m["x"] = np.ascontiguousarray(x[NB * i:NB * (i + 1)])
        m["mk"] = np.ascontiguousarray(mk[NB * i:NB * (i + 1)])
        in_maps.append(m)
    res = run_bass_kernel_spmd(nc, in_maps, core_ids=list(range(NCORES)))
    out = np.concatenate(
        [np.asarray(r["out"], dtype=np.float32) for r in res.results], axis=0)
    return out


# version 21
# speedup vs baseline: 1.0127x; 1.0127x over previous
import os

import numpy as np
import concourse.bass as bass
import concourse.bacc as bacc
import concourse.mybir as mybir
import concourse.dve_ops as DOPS
from concourse.dve_op import DveOp
from concourse.dve_spec import Spec, Src0, Src1, C0, C2, sq, select, lower
from concourse.bass_utils import run_bass_kernel_spmd

F32 = mybir.dt.float32
BF16 = mybir.dt.bfloat16
F8 = mybir.dt.float8e4
DR = mybir.MatmulPerfMode.DoubleRow
AF = mybir.ActivationFunctionType
ALU = mybir.AluOpType
AX = mybir.AxisListType

NB = 2
TT = 16
T = TT * 128
D = 1024
DC = D // 128
NH = 16
NP = NH // 2
HF = 4096
HT = HF // 128
QH = 1024
NCORES = 8

C_SCALE8, C_NEGLN8, C_HALF, C_NEGHALF, C_EPS, C_NEG1 = range(6)
NCST = 8


def _make_consts():
    c = np.zeros((128, NCST), dtype=np.float32)
    c[:, C_SCALE8] = 0.125
    c[:, C_NEGLN8] = np.float32(-np.log(8.0))
    c[:, C_HALF] = 0.5
    c[:, C_NEGHALF] = -0.5
    c[:, C_EPS] = 1e-5
    c[:, C_NEG1] = -1.0
    return c


_EXP2_HEX = [
    '0x1.0000000000008p-60', '0x1.0000000000011p-59', '0x1.ffffffffffff2p-59',
    '0x1.0000000000001p-57', '0x1.000000000000ap-56', '0x1.0000000000012p-55',
    '0x1.ffffffffffff5p-55', '0x1.0000000000003p-53', '0x1.000000000000bp-52',
    '0x1.0000000000014p-51', '0x1.ffffffffffff8p-51', '0x1.0000000000005p-49',
    '0x1.000000000000dp-48', '0x1.fffffffffffebp-48', '0x1.ffffffffffffcp-47',
    '0x1.0000000000006p-45', '0x1.ffffffffffffdp-45', '0x1.0000000000007p-43',
    '0x1.fffffffffffffp-43', '0x1.0000000000008p-41', '0x1.0000000000000p-40',
    '0x1.0000000000009p-39', '0x1.0000000000001p-38', '0x1.0000000000009p-37',
    '0x1.0000000000002p-36', '0x1.000000000000ap-35', '0x1.0000000000003p-34',
    '0x1.000000000000bp-33', '0x1.0000000000003p-32', '0x1.ffffffffffff7p-32',
    '0x1.0000000000004p-30', '0x1.ffffffffffff9p-30', '0x1.0000000000005p-28',
    '0x1.ffffffffffffbp-28', '0x1.0000000000006p-26', '0x1.ffffffffffffcp-26',
    '0x1.0000000000007p-24', '0x1.ffffffffffffep-24', '0x1.fffffffffffffp-23',
    '0x1.fffffffffffffp-22', '0x1.0000000000000p-20', '0x1.0000000000000p-19',
    '0x1.0000000000001p-18', '0x1.0000000000001p-17', '0x1.0000000000002p-16',
    '0x1.0000000000002p-15', '0x1.0000000000002p-14', '0x1.0000000000003p-13',
    '0x1.0000000000003p-12', '0x1.fffffffffffffp-12', '0x1.0000000000000p-10',
    '0x1.0000000000000p-9', '0x1.0000000000001p-8', '0x1.0000000000001p-7',
    '0x1.0000000000002p-6', '0x1.0000000000000p-5', '0x1.0000000000000p-4',
    '0x1.0000000000001p-3', '0x1.0000000000000p-2', '0x1.0000000000000p-1',
    '0x1.0000000000000p+0', '0x1.0000000000000p+1', '0x1.0000000000000p+2',
    '0x1.ffffffffffffep+2', '0x1.fffffffffffffp+3', '0x1.0000000000000p+5',
    '0x1.ffffffffffffdp+5', '0x1.ffffffffffffep+6', '0x1.ffffffffffffep+7',
    '0x1.fffffffffffffp+8', '0x1.0000000000000p+10', '0x1.0000000000000p+11',
    '0x1.ffffffffffff9p+11', '0x1.ffffffffffffap+12', '0x1.ffffffffffffbp+13',
    '0x1.ffffffffffffcp+14', '0x1.ffffffffffffdp+15', '0x1.ffffffffffffdp+16',
    '0x1.ffffffffffffep+17', '0x1.fffffffffffffp+18', '0x1.0000000000000p+20',
    '0x1.0000000000000p+21', '0x1.0000000000001p+22', '0x1.0000000000001p+23',
    '0x1.ffffffffffff3p+23', '0x1.0000000000002p+25', '0x1.ffffffffffff5p+25',
    '0x1.0000000000003p+27', '0x1.ffffffffffff6p+27', '0x1.0000000000003p+29',
    '0x1.ffffffffffff8p+29', '0x1.0000000000004p+31', '0x1.ffffffffffff9p+31',
    '0x1.fffffffffffeap+32', '0x1.ffffffffffffbp+33', '0x1.fffffffffffecp+34',
    '0x1.ffffffffffffcp+35', '0x1.fffffffffffedp+36', '0x1.ffffffffffffep+37',
    '0x1.fffffffffffefp+38', '0x1.0000000000000p+40',
]
_EXP2_LUT = np.array([float.fromhex(h) for h in _EXP2_HEX])


def _np_float_quantize(x, exp_bits, man_bits):
    xd = np.asarray(x).astype(np.float64)
    bias = 2 ** (exp_bits - 1) - 1
    min_e = 1 - bias
    _, e = np.frexp(xd)
    ue = np.maximum(e - 1, min_e)
    kk = (ue - man_bits).astype(np.int64)
    kc = np.clip(kk, -60, 40)
    scale = _EXP2_LUT[kc + 60]
    scale = np.where(kk == kc, scale, np.exp2(kk.astype(np.float64)))
    q = np.round(xd / scale) * scale
    return np.where(xd == 0, np.zeros_like(xd), q)


_QPARAMS = {
    "ANTQ_E4M3": (4, 3, 20, 1.5 * 2 ** 14, 2.0 ** -6),
    "ANTQ_FP16": (5, 10, 13, 0.75, 2.0 ** -14),
}
_QREG = {}


def _register_q(name):
    if name in _QREG:
        return _QREG[name]
    for op in DOPS.OPS:
        if op.name == name:
            _QREG[name] = op
            return op
    EW, MW, s, K, thresh = _QPARAMS[name]
    grid = 2.0 ** (1 - (2 ** (EW - 1) - 1) - MW)
    t = Src0 * C0
    body = select(sq(Src0) >= C2, t - (t - Src0), Src1)

    def ref(in0, in1, s0, s1, imm2, _EW=EW, _MW=MW):
        return _np_float_quantize(in0, _EW, _MW)

    spec = Spec(body=body)
    shas = {}
    for ver in ("v3", "v4"):
        try:
            shas[ver] = lower(spec, ver=ver, perf_en=False, name=name).sha(ver)
        except Exception:
            pass
    op = DveOp(name, spec, subdim=False, uops_sha=shas, reference=ref)
    DOPS.OPS.append(op)
    DOPS.CUSTOM_DVE_SPECS[name] = spec
    DOPS.CUSTOM_DVE_REFS[name] = ref
    DOPS._SUB_OPCODE_FOR_NAME[name] = DOPS._CUSTOM_DVE_ROW_BASE + len(DOPS.OPS) - 1
    assert DOPS._SUB_OPCODE_FOR_NAME[name] < 0x20
    _QREG[name] = op
    return op


class Res:
    __slots__ = ("name", "writer", "readers")

    def __init__(self, name=""):
        self.name = name
        self.writer = None
        self.readers = []


class _Rec:
    __slots__ = ("calls",)

    def __init__(self):
        self.calls = []

    def __getattr__(self, name):
        def f(*a, **k):
            self.calls.append((name, a, k))
        return f


class Em:
    ENG = ("pe", "act", "dve", "pool", "sp")

    def __init__(self, nc):
        self.nc = nc
        self._ctx = []
        self.sems = {}
        self.cnt = {}
        self.waited = {e: {} for e in self.ENG}
        self.prog = {e: [] for e in self.ENG}
        for q in self.ENG:
            self._new_sem(q)

    def _new_sem(self, q):
        cm = self.nc.semaphore("sem_" + q)
        self.sems[q] = cm.__enter__()
        self._ctx.append(cm)
        self.cnt[q] = 0

    def dma_sem(self, q):
        if q not in self.sems:
            self._new_sem(q)
        return q

    def sbuf(self, name, shape, dt=F32):
        cm = self.nc.sbuf_tensor(name, list(shape), dt)
        t = cm.__enter__()
        self._ctx.append(cm)
        return t, Res(name)

    def op(self, e, build, ins=(), outs=(), dma=None):
        need = {}
        for r in ins:
            if r.writer:
                s, c = r.writer
                need[s] = max(need.get(s, 0), c)
        for r in outs:
            if r.writer:
                s, c = r.writer
                need[s] = max(need.get(s, 0), c)
            for s, c in r.readers:
                need[s] = max(need.get(s, 0), c)
        w = self.waited[e]
        waits = [(s, c) for s, c in sorted(need.items()) if w.get(s, 0) < c]
        for s, c in waits:
            w[s] = c
        if dma is not None:
            self.cnt[dma] += 16
            tok = (dma, self.cnt[dma])
            inc = (dma, 16)
        else:
            self.cnt[e] += 1
            tok = (e, self.cnt[e])
            inc = (e, 1)
        rec = _Rec()
        build(rec)

        def bound(E, _calls=rec.calls):
            inst = None
            for name, a, k in _calls:
                inst = getattr(E, name)(*a, **k)
            return inst

        self.prog[e].append((waits, bound, inc))
        for r in outs:
            r.writer = tok
            r.readers = []
        for r in ins:
            r.readers.append(tok)
        return tok

    def wait(self, e, q, cnt):
        if self.waited[e].get(q, 0) >= cnt:
            return
        self.prog[e].append(([(q, cnt)], None, None))
        self.waited[e][q] = cnt

    def flush(self):
        with self.nc.Block() as block:
            def emit_stream(e, E):
                for waits, build, inc in self.prog[e]:
                    for s, c in waits:
                        E.wait_ge(self.sems[s], c)
                    if build is not None:
                        inst = build(E)
                        if inc is not None:
                            inst.then_inc(self.sems[inc[0]], inc[1])

            @block.tensor
            def _(E):
                emit_stream("pe", E)

            @block.scalar
            def _(E):
                emit_stream("act", E)

            @block.vector
            def _(E):
                emit_stream("dve", E)

            @block.gpsimd
            def _(E):
                emit_stream("pool", E)

            @block.sync
            def _(E):
                emit_stream("sp", E)

    def close(self):
        for cm in reversed(self._ctx):
            cm.__exit__(None, None, None)


def emit_quant(em, opname, out_ap, in_ap, ins, outs, scr_ap, scr_res, e="dve"):
    op = _register_q(opname)
    EW, MW, s, K, thresh = _QPARAMS[opname]
    Cc = float(2.0 ** s + 1.0)
    em.op(e, lambda E: E.tensor_scalar(scr_ap, in_ap, float(K), float(K),
                                       ALU.add, ALU.subtract),
          ins=list(ins), outs=[scr_res])
    return em.op(e, lambda E: E._custom_dve(op, out=out_ap, in0=in_ap,
                                            in1=scr_ap, s0=Cc,
                                            imm2=float(thresh * thresh)),
                 ins=list(ins) + [scr_res], outs=list(outs))


def emit_quant_actsnap(em, opname, out_ap, in_ap, ins, outs, scr_ap, scr_res):
    op = _register_q(opname)
    EW, MW, s, K, thresh = _QPARAMS[opname]
    Cc = float(2.0 ** s + 1.0)
    em.op("act", lambda E: E.activation(scr_ap, in_ap, AF.Copy,
                                        bias=float(K), scale=1.0),
          ins=list(ins), outs=[scr_res])
    em.op("act", lambda E: E.activation(scr_ap, scr_ap, AF.Copy,
                                        bias=float(-K), scale=1.0),
          ins=[scr_res], outs=[scr_res])
    return em.op("dve", lambda E: E._custom_dve(op, out=out_ap, in0=in_ap,
                                                in1=scr_ap, s0=Cc,
                                                imm2=float(thresh * thresh)),
                 ins=list(ins) + [scr_res], outs=list(outs))


def build_program(nc):
    din = lambda n, s, dt=F32: nc.dram_tensor(n, list(s), dt,
                                              kind="ExternalInput").ap()
    x_in = din("x", [NB, T, D])
    mask_in = din("mk", [NB, T, D], BF16)
    Wqkv = din("wqkv", [D, 3 * D])
    bqkv_in = din("bqkv", [1, 3 * D])
    Wproj = din("wproj", [D, D])
    bproj_in = din("bproj", [1, D])
    w1T = din("w1t", [D, HF])
    b1_in = din("b1", [HT, 128])
    w2T = din("w2t", [HF, D])
    b2_in = din("b2", [1, D])
    gamma_in = din("gam", [DC, 128])
    beta_in = din("bet", [DC, 128])
    ident_in = din("idn", [128, 128], BF16)
    identf_in = din("idf", [128, 128])
    cst_in = din("cst", [128, NCST])

    w1q_sc = nc.dram_tensor("w1qs", [D, HF], F8, kind="Internal").ap()
    w2q_sc = nc.dram_tensor("w2qs", [HF, D], F8, kind="Internal").ap()
    out_hbm = nc.dram_tensor("out", [NB, T, D], F32, kind="ExternalOutput").ap()

    em = Em(nc)
    S = em.sbuf

    pcm = nc.psum_tensor("pp", [128, 3584], F32)
    pp = pcm.__enter__()
    em._ctx.append(pcm)
    pbcm = nc.psum_tensor("ppb", [128, 1024], BF16)
    ppb = pbcm.__enter__()
    em._ctx.append(pbcm)
    pbank = [Res(f"pb{i}") for i in range(7)]
    ppb_r = Res("ppb")

    def pb(lo, hi):
        return [pbank[i] for i in range(lo // 512, (hi + 511) // 512)]

    A, A_r = S("arnA", (128, 16384), BF16)
    B, B_r = S("arnB", (128, 16384), BF16)
    C, C_r = S("arnC", (128, 16640), BF16)
    W, W_r = S("arnW", (128, 24576), BF16)
    Whi_r = Res("arnWhi")
    FA, FA_r = S("arnF", (128, 4096))

    def addr_of(name):
        for a in nc.m.functions[0].allocations:
            mls = getattr(a, "memorylocations", None)
            if not mls:
                continue
            if mls[0].name == name:
                return mls[0].addr
        raise KeyError(name)

    qk_sb = nc.alloc_sbuf_tensor_at("qksb", [128, 2048], BF16,
                                    offset=addr_of("arnF") + 12288)
    yfull = nc.alloc_sbuf_tensor_at("yfull", [128, 1024], F32,
                                    offset=addr_of("arnA") + 16384)
    yq = nc.alloc_sbuf_tensor_at("yq", [128, 1024], F32,
                                 offset=addr_of("arnA") + 20480)
    bproj_sb = nc.alloc_sbuf_tensor_at("bprojs", [1, 1024], F32,
                                       offset=addr_of("arnC") + 20480)
    b2q = nc.alloc_sbuf_tensor_at("b2q", [1, 1024], F32,
                                  offset=addr_of("arnC") + 24576)
    qk_sb_r = Res("qksb")
    yfull_r = Res("yfull")
    yq_r = Res("yq")
    bproj_r = Res("bprojs")
    b2q_r = Res("b2q")

    QTb = A[:, 0:NP * T]
    xn2qT = A[:, 0:DC * 512]
    KTb = B[:, 0:NP * T]
    hTb = B[:, 0:HT * 512]
    Vb = C[:, 0:TT * NH * 65]
    wp = lambda c: C[:, c * 1024:(c + 1) * 1024]
    w2b = lambda i: C[:, 8192 + i * 1024:8192 + (i + 1) * 1024]
    wq = lambda c: W[:, c * 3072:(c + 1) * 3072]

    def wq_res(c):
        lo, hi = c * 3072, (c + 1) * 3072
        rs = []
        if lo < 16384:
            rs.append(W_r)
        if hi > 16384:
            rs.extend(whiR)
        return rs

    ctx_all = W[:, 0:NP * T]
    w1h = W[:, 16384:24576]
    wf = FA[:, 0:3072]
    x2sl = lambda i: FA[:, (i % 4) * 1024:((i % 4) + 1) * 1024]

    cst, cst_r = S("cstt", (128, NCST))
    ident, ident_r = S("ident", (128, 128), BF16)
    identf, identf_r = S("identf", (128, 128))
    ones128b, ones128b_r = S("ones128b", (128, 1), BF16)
    onesb1, onesb1_r = S("onesb1", (1, 128), BF16)
    bcombb, bcombb_r = S("bcombb", (1, 3072), BF16)
    bprojb, bprojb_r = S("bprojb", (1, 1024), BF16)
    b2qb, b2qb_r = S("b2qb", (1, 1024), BF16)
    recipb, recipb_r = S("recipb", (1, 1024), BF16)
    qqkkb, qqkkb_r = S("qqkkb", (1, NH), BF16)
    gam, gam_r = S("gamt", (128, DC))
    bet, bet_r = S("bett", (128, DC))
    b1q, b1q_r = S("b1q", (128, HT))
    rowA, rowA_r = S("rowA", (1, 2080))
    mneg, mneg_r = S("mneg", (128, NH))
    xt, xt_r = S("xt", (128, D))
    xn1, xn1_r = S("xn1", (128, D))
    xn2, xn2_r = S("xn2", (128, D), BF16)
    PT0, PT0_r = S("PT0", (128, QH), BF16)
    PT1, PT1_r = S("PT1", (128, QH), BF16)
    scr, scr_r = S("scr", (128, D))
    mu, mu_r = S("mu", (128, 16))
    gtmp, gtmp_r = S("gtmp", (32, 128))
    gq, gq_r = S("gq", (32, 128))
    gscr, gscr_r = S("gscr", (32, 128))
    Bsb = nc.alloc_sbuf_tensor_at("Bsb", [128, QH], F32, offset=addr_of("xt"))
    mk_sb = nc.alloc_sbuf_tensor_at("mksb", [128, D], BF16,
                                    offset=addr_of("xn2"))
    htmp = nc.alloc_sbuf_tensor_at("htmp", [128, 512], F32,
                                   offset=addr_of("xn1"))
    sl2 = nc.alloc_sbuf_tensor_at("sl2", [128, 512], F32,
                                  offset=addr_of("xn1") + 2048)
    recipz = nc.alloc_sbuf_tensor_at("recipz", [128, QH], BF16,
                                     offset=addr_of("xn2"))
    Bsb_r = xt_r
    mk_sb_r = xn2_r
    htmp_r = xn1_r
    sl2_r = xn1_r
    recipz_r = xn2_r
    mred = nc.alloc_sbuf_tensor_at("mred", [128, 64], F32,
                                   offset=addr_of("xn2"))
    mred_r = xn2_r
    onesb512 = nc.alloc_sbuf_tensor_at("onesb512", [1, 512], BF16,
                                       offset=addr_of("gtmp"))
    onesK = nc.alloc_sbuf_tensor_at("onesK", [128, 128], BF16,
                                    offset=addr_of("identf"))
    mm4 = nc.alloc_sbuf_tensor_at("mm4", [128, 128], BF16,
                                  offset=addr_of("gscr"))
    mm4_r = gscr_r
    xnT2 = [nc.alloc_sbuf_tensor_at("xnT2a", [128, 2048], BF16,
                                    offset=addr_of("PT0")),
            nc.alloc_sbuf_tensor_at("xnT2b", [128, 2048], BF16,
                                    offset=addr_of("scr"))]
    xnT2R = [[PT0_r, PT1_r], [scr_r]]
    xn2x4 = nc.alloc_sbuf_tensor_at("xn2x4", [128, 4096], BF16,
                                    offset=addr_of("arnB") + 16384)
    xn2x4_r = Res("xn2x4")
    wpstg = nc.alloc_sbuf_tensor_at("wpstg", [128, 1024], F32,
                                    offset=addr_of("arnB"))
    wpsR2 = [Res("wps0"), Res("wps1")]
    xn2b = nc.alloc_sbuf_tensor_at("xn2b", [128, D], BF16,
                                   offset=addr_of("recipb"))
    xn2b_r = Res("xn2b")
    wfC = [nc.alloc_sbuf_tensor_at("wfC0", [128, 1536], F32,
                                   offset=addr_of("arnC")),
           nc.alloc_sbuf_tensor_at("wfC1", [128, 1536], F32,
                                   offset=addr_of("arnC") + 6144)]
    wfcR = [Res("wfc0"), Res("wfc1")]
    assert addr_of("PT1") == addr_of("PT0") + 2048
    assert addr_of("gq") == addr_of("gtmp") + 512
    b8 = nc.alloc_sbuf_tensor_at("b8", [128, 4096], F8, offset=addr_of("arnB"))
    xn2qT8 = nc.alloc_sbuf_tensor_at("xn2qT8", [128, 4096], F8,
                                     offset=addr_of("arnA"))
    hTbf8 = nc.alloc_sbuf_tensor_at("hTbf8", [128, HT * 512], F8,
                                    offset=addr_of("arnB"))
    w1q8 = [nc.alloc_sbuf_tensor_at("w1q8a", [128, 8192], F8,
                                    offset=addr_of("arnW") + 32768),
            nc.alloc_sbuf_tensor_at("w1q8b", [128, 8192], F8,
                                    offset=addr_of("arnW") + 40960)]
    W1R = [Res("w1q8a"), Res("w1q8b")]
    whiR = W1R
    w2bf8 = [nc.alloc_sbuf_tensor_at("w2bf8a", [128, 2048], F8,
                                     offset=addr_of("arnC") + 16384),
             nc.alloc_sbuf_tensor_at("w2bf8b", [128, 2048], F8,
                                     offset=addr_of("arnC") + 18432),
             nc.alloc_sbuf_tensor_at("w2bf8c", [128, 2048], F8,
                                     offset=addr_of("arnC") + 28672),
             nc.alloc_sbuf_tensor_at("w2bf8d", [128, 2048], F8,
                                     offset=addr_of("arnC") + 30720)]

    x2R = [Res(f"x2_{i}") for i in range(4)]
    w2bR2 = [[Res(f"w2b{i}l"), Res(f"w2b{i}h")] for i in range(4)]
    w2bR = [r for pair in w2bR2 for r in pair]
    PT = [(PT0, PT0_r), (PT1, PT1_r)]
    sumr = rowA[0:1, 0:1024]
    recip = rowA[0:1, 1024:2048]
    qqkk = rowA[0:1, 2048:2048 + 2 * NH]

    def dma(sem_name, dst_ap, src_ap, ins=(), outs=()):
        sem = em.dma_sem(sem_name)
        return em.op("sp", lambda E: E.dma_start(out=dst_ap, in_=src_ap),
                     ins=list(ins), outs=list(outs), dma=sem)

    dma("ld0", cst[:], cst_in, outs=[cst_r])
    dma("ld1", ident[:], ident_in, outs=[ident_r])
    dma("ld2", identf[:], identf_in, outs=[identf_r])
    em.op("dve", lambda E: E.memset(ones128b[:], 1.0), outs=[ones128b_r])
    em.op("dve", lambda E: E.memset(onesb1[:], 1.0), outs=[onesb1_r])
    tl_inst = mybir.InstLoadActFuncSet(
        name=nc.get_next_instruction_name(), act_func_set_id=6, ins=[], outs=[])
    em.op("act", lambda E: E.add_instruction(tl_inst))

    def load_cols(dst_ap, dst_r, src, ncols, quant):
        dma("ldg", gtmp[0:ncols, :], src, outs=[gtmp_r])
        sb_, sr_ = gtmp, gtmp_r
        if quant:
            emit_quant(em, "ANTQ_E4M3", gq[0:ncols, :], gtmp[0:ncols, :],
                       [gtmp_r], [gq_r], gscr[0:ncols, :], gscr_r)
            sb_, sr_ = gq, gq_r
        em.op("pe", lambda E: E.transpose(pp[0:128, 3072:3072 + ncols],
                                          sb_[0:ncols, :],
                                          identf[0:ncols, 0:ncols]),
              ins=[sr_, identf_r], outs=pb(3072, 3584))
        em.op("dve", lambda E: E.tensor_copy(dst_ap, pp[0:128, 3072:3072 + ncols]),
              ins=pb(3072, 3584), outs=[dst_r])

    load_cols(gam[:, 0:DC], gam_r, gamma_in, DC, False)
    load_cols(bet[:, 0:DC], bet_r, beta_in, DC, False)
    load_cols(b1q[:, 0:HT], b1q_r, b1_in, HT, True)
    em.op("dve", lambda E: E.memset(onesb512[:], 1.0),
          outs=[gtmp_r, gq_r])
    em.op("dve", lambda E: E.memset(onesK[:], 1.0), outs=[identf_r])

    wfR = [Res("wf0"), Res("wf1")]
    wfh = [FA[:, 0:1536], FA[:, 1536:3072]]

    def emit_wq_stream(b):
        if b == 0:
            bufs, bR, extra = wfh, wfR, [FA_r]
        else:
            bufs, bR, extra = [wfC[0][:], wfC[1][:]], wfcR, [C_r]
        for c in range(DC):
            for hf in range(2):
                em.op("sp", lambda E, c=c, hf=hf, bufs=bufs: E.dma_start(
                    out=bufs[hf],
                    in_=Wqkv[c * 128:(c + 1) * 128,
                             hf * 1536:(hf + 1) * 1536]),
                    outs=[bR[hf]] + extra,
                    dma=em.dma_sem(f"wqs{(2 * c + hf) % 4}"))
                if b == 0:
                    for nb_ in range(3):
                        lo = hf * 1536 + nb_ * 512
                        em.op("pe", lambda E, c=c, hf=hf, nb_=nb_, lo=lo,
                              bufs=bufs: E.matmul(
                                  pp[0:1, lo:lo + 512],
                                  bet[:, c:c + 1],
                                  bufs[hf][:, nb_ * 512:(nb_ + 1) * 512],
                                  start=(c == 0), stop=(c == DC - 1)),
                              ins=[bet_r, bR[hf]], outs=pb(lo, lo + 512))
                em.op("dve", lambda E, c=c, hf=hf, bufs=bufs: E.tensor_scalar(
                    wq(c)[:, hf * 1536:(hf + 1) * 1536], bufs[hf],
                    gam[:, c:c + 1], None, ALU.mult),
                    ins=[bR[hf], gam_r] + extra, outs=wq_res(c))
        if b == 0:
            dma("ldbq", scr[0:96:32, 0:1024],
                bqkv_in.rearrange("a (b c) -> (a b) c", c=1024),
                outs=[scr_r])
            for j in range(3):
                em.op("dve", lambda E, j=j: E.tensor_tensor(
                    bcombb[0:1, j * 1024:(j + 1) * 1024],
                    pp[0:1, j * 1024:(j + 1) * 1024],
                    scr[32 * j:32 * j + 1, 0:1024], ALU.add),
                    ins=pb(j * 1024, (j + 1) * 1024) + [scr_r],
                    outs=[bcombb_r])

    emit_wq_stream(0)

    p0f = nc.alloc_sbuf_tensor_at("p0f", [128, 4096], F32,
                                  offset=addr_of("arnC"))
    p0scr = nc.alloc_sbuf_tensor_at("p0scr", [128, 1024], F32,
                                    offset=addr_of("xn1"))
    p0R = [Res("p0_0"), Res("p0_1")]
    b8R = [Res("b8_0"), Res("b8_1")]
    spillR = Res("p0spill")
    spillR1 = Res("p0spillw1")

    def emit_p0():
        for c in range(DC):
            for piece in range(2):
                sl = piece % 2
                em.op("sp", lambda E, c=c, piece=piece, sl=sl: E.dma_start(
                    out=p0f[:, sl * 2048:sl * 2048 + 2048],
                    in_=w1T[c * 128:(c + 1) * 128,
                            piece * 2048:(piece + 1) * 2048]),
                    outs=[p0R[sl]],
                    dma=em.dma_sem(f"p0a{(2 * c + piece) % 4}"))
                for qq in range(2):
                    emit_quant(em, "ANTQ_E4M3",
                               b8[:, (2 * piece + qq) * 1024:
                                  (2 * piece + qq + 1) * 1024],
                               p0f[:, (sl * 2 + qq) * 1024:
                                   (sl * 2 + qq + 1) * 1024],
                               [p0R[sl]], [b8R[sl]], p0scr[:], xn1_r)
                em.op("sp", lambda E, c=c, piece=piece, sl=sl: E.dma_start(
                    out=w1q_sc[c * 128:(c + 1) * 128,
                               piece * 2048:(piece + 1) * 2048],
                    in_=b8[:, piece * 2048:(piece + 1) * 2048]),
                    ins=[b8R[sl]], outs=[spillR1],
                    dma=em.dma_sem(f"p0o{sl}"))
        for c in range(HT):
            sl = c % 2
            em.op("sp", lambda E, c=c, sl=sl: E.dma_start(
                out=p0f[:, sl * 1024:sl * 1024 + 1024],
                in_=w2T[c * 128:(c + 1) * 128, :]),
                outs=[p0R[sl]], dma=em.dma_sem(f"p0c{c % 4}"))
            emit_quant(em, "ANTQ_E4M3", b8[:, sl * 1024:sl * 1024 + 1024],
                       p0f[:, sl * 1024:sl * 1024 + 1024],
                       [p0R[sl]], [b8R[sl]], p0scr[:], xn1_r)
            em.op("sp", lambda E, c=c, sl=sl: E.dma_start(
                out=w2q_sc[c * 128:(c + 1) * 128, :],
                in_=b8[:, sl * 1024:sl * 1024 + 1024]),
                ins=[b8R[sl]], outs=[spillR], dma=em.dma_sem(f"p0d{sl}"))

    for b in range(NB):
        if b > 0:
            emit_wq_stream(b)

        def vb_memset():
            em.op("dve", lambda E: E.memset(Vb, 1.0),
                  outs=[C_r, B_r, xn2x4_r] + w2bR + p0R)

        def emit_ln(src_ap, src_ins, out_ap, out_r, dump_ap, dump_r, ms):
            c0 = 8 * ms
            em.op("act", lambda E: E.activation(
                dump_ap, src_ap, AF.Copy, bias=0.0, scale=1.0,
                accum_out=mu[:, c0:c0 + 1]),
                ins=list(src_ins), outs=[dump_r, mu_r])
            em.op("act", lambda E: E.activation(
                dump_ap, src_ap, AF.Square, bias=0.0, scale=1.0,
                accum_out=mu[:, c0 + 1:c0 + 2]),
                ins=list(src_ins), outs=[dump_r, mu_r])
            em.op("dve", lambda E: E.tensor_scalar(
                mu[:, c0 + 2:c0 + 3], mu[:, c0:c0 + 1], 1.0 / D, None,
                ALU.mult), ins=[mu_r], outs=[mu_r])
            em.op("dve", lambda E: E.tensor_scalar(
                mu[:, c0 + 3:c0 + 4], mu[:, c0 + 1:c0 + 2], 1.0 / D, None,
                ALU.mult), ins=[mu_r], outs=[mu_r])
            em.op("dve", lambda E: E.tensor_tensor(
                mu[:, c0 + 4:c0 + 5], mu[:, c0 + 2:c0 + 3],
                mu[:, c0 + 2:c0 + 3], ALU.mult),
                ins=[mu_r], outs=[mu_r])
            em.op("dve", lambda E: E.tensor_tensor(
                mu[:, c0 + 5:c0 + 6], mu[:, c0 + 3:c0 + 4],
                mu[:, c0 + 4:c0 + 5], ALU.subtract),
                ins=[mu_r], outs=[mu_r])
            em.op("act", lambda E: E.activation(
                mu[:, c0 + 6:c0 + 7], mu[:, c0 + 5:c0 + 6], AF.Ln,
                bias=cst[:, C_EPS:C_EPS + 1], scale=1.0),
                ins=[mu_r, cst_r], outs=[mu_r])
            em.op("act", lambda E: E.activation(
                mu[:, c0 + 7:c0 + 8], mu[:, c0 + 6:c0 + 7], AF.Exp, bias=0.0,
                scale=cst[:, C_NEGHALF:C_NEGHALF + 1]),
                ins=[mu_r, cst_r], outs=[mu_r])
            em.op("dve", lambda E: E.tensor_scalar(
                out_ap, src_ap, mu[:, c0 + 2:c0 + 3], mu[:, c0 + 7:c0 + 8],
                ALU.subtract, ALU.mult),
                ins=list(src_ins) + [mu_r], outs=[out_r])

        def p1_tiles_ln(g):
            for tl in range(2):
                ts = 2 * g + tl
                sl = x2sl(ts)
                slr = x2R[ts % 4]
                xdeps = [slr, FA_r, wfR[0], wfR[1]] if b == 0 else [slr]
                dma(f"xin{ts % 4}", sl, x_in[b, ts * 128:(ts + 1) * 128, :],
                    outs=xdeps)
                ms = ts % 2
                emit_ln(sl, [slr], xn1[:], xn1_r, xn1[:], xn1_r, ms)
                dst = xn2 if tl == 0 else xn2b
                dstr = xn2_r if tl == 0 else xn2b_r
                emit_ln(xn1[:], [xn1_r], dst[:], dstr, dst[:], dstr, ms)

        def p1_tiles_tr(g):
            buf = g % 2
            for tl in range(2):
                src = xn2 if tl == 0 else xn2b
                srcr = xn2_r if tl == 0 else xn2b_r
                for c in range(DC):
                    em.op("pe", lambda E, c=c, src=src: E.transpose(
                        ppb[:, c * 128:(c + 1) * 128],
                        src[:, c * 128:(c + 1) * 128], ident[:]),
                        ins=[srcr, ident_r], outs=[ppb_r])
                xview = xnT2[buf].rearrange("p (c t) -> p c t", t=256)[
                    :, :, tl * 128:(tl + 1) * 128]
                em.op("dve", lambda E, xview=xview: E.tensor_copy(
                    xview,
                    ppb[:, 0:1024].rearrange("p (c t) -> p c t", t=128)),
                    ins=[ppb_r], outs=list(xnT2R[buf]))

        def p1_group_mms(g):
            buf = g % 2
            xb = xnT2[buf]
            xbR = list(xnT2R[buf])
            for j in range(16):
                off = j * 128 if j < 8 else 1024 + (j - 8) * 128
                bank = (j % 4) * 512
                for c in range(DC):
                    em.op("pe", lambda E, c=c, off=off, bank=bank: E.matmul(
                        pp[:, bank:bank + 256],
                        wq(c)[:, off:off + 128],
                        xb[:, c * 256:(c + 1) * 256],
                        start=(c == 0), stop=False),
                        ins=xbR + wq_res(c), outs=pb(bank, bank + 256))
                em.op("pe", lambda E, off=off, bank=bank: E.matmul(
                    pp[:, bank:bank + 256],
                    bcombb[0:1, off:off + 128], onesb512[0:1, 0:256],
                    start=False, stop=True),
                    ins=[bcombb_r, gtmp_r, gq_r], outs=pb(bank, bank + 256))
                dst, dstr = (QTb, A_r) if j < 8 else (KTb, B_r)
                col = (j if j < 8 else j - 8) * T + g * 256
                evac_ins = pb(bank, bank + 256) + (
                    [spillR, spillR1] if j == 8 else [])
                em.op("act", lambda E, dst=dst, col=col, bank=bank:
                      E.activation(dst[:, col:col + 256],
                                   pp[:, bank:bank + 256],
                                   AF.Copy, bias=0.0, scale=1.0),
                      ins=evac_ins, outs=[dstr])
                em.op("dve", lambda E, dst=dst, col=col, g=g, j=j:
                      E.tensor_reduce(mm4[:, g * 16 + j:g * 16 + j + 1],
                                      dst[:, col:col + 256], AX.X, ALU.max,
                                      apply_absolute_value=True),
                      ins=[dstr], outs=[mm4_r])
            for tl in range(2):
                ts = 2 * g + tl
                for c in range(DC):
                    for nb_ in range(2):
                        em.op("pe", lambda E, c=c, tl=tl, nb_=nb_: E.matmul(
                            pp[:, 2048 + nb_ * 512:2048 + (nb_ + 1) * 512],
                            xb[:, c * 256 + tl * 128:c * 256 + tl * 128 + 128],
                            wq(c)[:, 2048 + nb_ * 512:2048 + (nb_ + 1) * 512],
                            start=(c == 0), stop=False),
                            ins=xbR + wq_res(c),
                            outs=pb(2048 + nb_ * 512, 2048 + (nb_ + 1) * 512))
                for nb_ in range(2):
                    em.op("pe", lambda E, nb_=nb_: E.matmul(
                        pp[:, 2048 + nb_ * 512:2048 + (nb_ + 1) * 512],
                        onesb1[0:1, 0:128],
                        bcombb[0:1, 2048 + nb_ * 512:2048 + (nb_ + 1) * 512],
                        start=False, stop=True),
                        ins=[onesb1_r, bcombb_r],
                        outs=pb(2048 + nb_ * 512, 2048 + (nb_ + 1) * 512))
                vdst = Vb[:, ts * NH * 65:(ts + 1) * NH * 65].rearrange(
                    "p (h c) -> p h c", c=65)[:, :, 0:64]
                vsrc = pp[:, 2048:3072].rearrange("p (h c) -> p h c", c=64)
                em.op("dve", lambda E, vdst=vdst, vsrc=vsrc: E.tensor_copy(
                    vdst, vsrc), ins=pb(2048, 3072), outs=[C_r])

        if b == 0:
            p1_tiles_ln(0)
            p1_tiles_tr(0)
            p1_tiles_ln(1)
            emit_p0()
            vb_memset()
            p1_tiles_tr(1)
            pre = 2
        else:
            vb_memset()
            p1_tiles_ln(0)
            p1_tiles_tr(0)
            pre = 1
        for g in range(TT // 2):
            if pre <= g + 1 < TT // 2:
                p1_tiles_ln(g + 1)
            p1_group_mms(g)
            if pre <= g + 1 < TT // 2:
                p1_tiles_tr(g + 1)

        mmv = mm4.rearrange("p (g j) -> p j g", j=16)
        em.op("dve", lambda E: E.tensor_reduce(
            mred[:, 0:16], mmv, AX.X, ALU.max),
            ins=[mm4_r], outs=[mred_r])
        em.op("dve", lambda E: E.tensor_tensor(
            mred[:, 16:32], mred[:, 0:16], mred[:, 0:16], ALU.mult),
            ins=[mred_r], outs=[mred_r])
        em.op("pool", lambda E: E.tensor_reduce(
            mred[0:1, 32:48], mred[0:64, 16:32], AX.C, ALU.add),
            ins=[mred_r], outs=[mred_r])
        em.op("pool", lambda E: E.tensor_reduce(
            mred[0:1, 48:64], mred[64:128, 16:32], AX.C, ALU.add),
            ins=[mred_r], outs=[mred_r])
        qkview = rowA[0:1, 2048:2080].rearrange("a (x h two) -> a x two h",
                                                two=2, h=8)
        for i2, c0 in enumerate((32, 48, 40, 56)):
            em.op("dve", lambda E, i2=i2, c0=c0: E.tensor_copy(
                qkview[:, i2 // 2, i2 % 2, :],
                mred[0:1, c0:c0 + 8]),
                ins=[mred_r], outs=[rowA_r])
        em.op("dve", lambda E: E.tensor_scalar(qqkk, qqkk, 1e-18, None,
                                               ALU.max),
              ins=[rowA_r], outs=[rowA_r])
        em.op("act", lambda E: E.activation(qqkk, qqkk, AF.Ln,
                                            bias=0.0, scale=1.0),
              ins=[rowA_r], outs=[rowA_r])
        em.op("dve", lambda E: E.tensor_tensor(
            qqkk[0:1, 0:NH], qqkk[0:1, 0:NH], qqkk[0:1, NH:2 * NH], ALU.add),
            ins=[rowA_r], outs=[rowA_r])
        em.op("act", lambda E: E.activation(
            qqkk[0:1, NH:2 * NH], qqkk[0:1, 0:NH], AF.Exp,
            bias=cst[0:1, C_NEGLN8:C_NEGLN8 + 1],
            scale=cst[0:1, C_HALF:C_HALF + 1]),
            ins=[rowA_r, cst_r], outs=[rowA_r])
        em.op("dve", lambda E: E.tensor_scalar(
            qqkkb[:], qqkk[0:1, NH:2 * NH], -1.0, 30.0, ALU.mult, ALU.add),
            ins=[rowA_r], outs=[qqkkb_r])
        em.op("pe", lambda E: E.matmul(
            pp[0:128, 3072:3072 + NH], onesb1[0:1, 0:128],
            qqkkb[:], start=True, stop=True),
            ins=[onesb1_r, qqkkb_r], outs=pb(3072, 3584))
        em.op("dve", lambda E: E.tensor_copy(mneg[:], pp[0:128, 3072:3072 + NH]),
              ins=pb(3072, 3584), outs=[mneg_r])

        em.op("dve", lambda E: E.memset(recipz[64:128, :], 0.0),
              outs=[recipz_r])
        sums_slots = [(Bsb[64:65, 512:1024], xt_r), (sl2[64:65, 0:512],
                                                     sl2_r)]

        def emit_scores(h, qq, kt):
            p, s2 = h // 2, h % 2
            base = 64 * s2
            q0 = qq * 512
            bank = (kt % 4) * 512
            em.op("pe", lambda E, kt=kt, bank=bank, p=p, base=base, q0=q0:
                  E.matmul(
                      pp[:, bank:bank + 512],
                      KTb[base:base + 64,
                          p * T + kt * 128:p * T + (kt + 1) * 128],
                      QTb[base:base + 64, p * T + q0:p * T + q0 + 512],
                      start=True, stop=True),
                  ins=[B_r, A_r], outs=pb(bank, bank + 512))

        def emit_exp_pair(h, st):
            lo = (st % 2) * 1024
            ptv, ptr = PT[st % 2]
            em.op("act", lambda E, lo=lo, ptv=ptv, h=h: E.activation(
                ptv[:], pp[:, lo:lo + 1024], AF.Exp,
                bias=mneg[:, h:h + 1],
                scale=cst[:, C_SCALE8:C_SCALE8 + 1]),
                ins=pb(lo, lo + 1024) + [mneg_r, cst_r],
                outs=[ptr])

        def emit_ctx(h, kt):
            ptv, ptr = PT[(kt // 2) % 2]
            pcol = (kt % 2) * 512
            vc = kt * NH * 65 + h * 65
            for half in range(2):
                hb = 64 * half
                em.op("pe", lambda E, kt=kt, hb=hb, vc=vc, ptv=ptv,
                      pcol=pcol, half=half: E.matmul(
                          pp[0:65, 2048 + half * 512:2560 + half * 512],
                          Vb[hb:hb + 64, vc:vc + 65],
                          ptv[hb:hb + 64, pcol:pcol + 512],
                          start=(kt == 0), stop=(kt == TT - 1)),
                      ins=[C_r, ptr],
                      outs=pb(2048 + half * 512, 2560 + half * 512))

        def emit_evac(h, qq, u):
            p, s2 = h // 2, h % 2
            base = 64 * s2
            q0 = qq * 512
            srow, srow_r = sums_slots[u % 2]
            em.op("dve", lambda E: E.tensor_copy(
                htmp[0:65, :], pp[0:65, 2560:3072]),
                ins=pb(2560, 3072), outs=[htmp_r])
            em.op("dve", lambda E, p=p, base=base, q0=q0: E.tensor_tensor(
                ctx_all[base:base + 64, p * T + q0:p * T + q0 + 512],
                pp[0:64, 2048:2560], htmp[0:64, :], ALU.add),
                ins=pb(2048, 2560) + [htmp_r], outs=[W_r])
            em.op("dve", lambda E, srow=srow: E.tensor_tensor(
                srow, pp[64:65, 2048:2560], htmp[64:65, :], ALU.add),
                ins=pb(2048, 2560) + [htmp_r], outs=[srow_r])

        def emit_tail(h, qq, u):
            p, s2 = h // 2, h % 2
            base = 64 * s2
            q0 = qq * 512
            srow, srow_r = sums_slots[u % 2]
            em.op("dve", lambda E, srow=srow: E.tensor_scalar(
                srow, srow, 1e-12, None, ALU.max),
                ins=[srow_r], outs=[srow_r])
            em.op("act", lambda E, srow=srow: E.activation(
                srow, srow, AF.Ln, bias=0.0, scale=1.0),
                ins=[srow_r], outs=[srow_r])
            em.op("act", lambda E, srow=srow: E.activation(
                recipz[64:65, 0:512], srow, AF.Exp, bias=0.0,
                scale=cst[0:1, C_NEG1:C_NEG1 + 1]),
                ins=[srow_r, cst_r], outs=[recipz_r])
            em.op("pe", lambda E: E.matmul(
                pp[0:128, 3072:3584], onesK[64:128, :],
                recipz[64:128, 0:512], start=True, stop=True),
                ins=[identf_r, recipz_r], outs=pb(3072, 3584))
            em.op("dve", lambda E: E.tensor_copy(
                Bsb[:, 0:512], pp[0:128, 3072:3584]),
                ins=pb(3072, 3584), outs=[Bsb_r])
            em.op("dve", lambda E, p=p, base=base, q0=q0: E.tensor_tensor(
                ctx_all[base:base + 64, p * T + q0:p * T + q0 + 512],
                ctx_all[base:base + 64, p * T + q0:p * T + q0 + 512],
                Bsb[base:base + 64, 0:512], ALU.mult),
                ins=[Bsb_r, W_r], outs=[W_r])

        def emit_ctx_half(h, kt, half):
            ptv, ptr = PT[(kt // 2) % 2]
            pcol = (kt % 2) * 512
            vc = kt * NH * 65 + h * 65
            hb = 64 * half
            em.op("pe", lambda E, kt=kt, hb=hb, vc=vc, ptv=ptv,
                  pcol=pcol, half=half: E.matmul(
                      pp[0:65, 2048 + half * 512:2560 + half * 512],
                      Vb[hb:hb + 64, vc:vc + 65],
                      ptv[hb:hb + 64, pcol:pcol + 512],
                      start=(kt == 0), stop=(kt == TT - 1)),
                  ins=[C_r, ptr],
                  outs=pb(2048 + half * 512, 2560 + half * 512))

        units = [(2 * p + s2_, qq) for p in range(NP)
                 for qq in range(4) for s2_ in range(2)]
        pend = None
        for u, (h, qq) in enumerate(units):
            sc_par = h % 2
            for st in range(TT // 2):
                if st >= 2:
                    k0, k1 = 2 * st - 4, 2 * st - 3
                    if sc_par == 0:
                        emit_ctx_half(h, k0, 1)
                        emit_scores(h, qq, 2 * st)
                        emit_ctx_half(h, k1, 1)
                        emit_scores(h, qq, 2 * st + 1)
                        emit_ctx_half(h, k0, 0)
                        emit_ctx_half(h, k1, 0)
                    else:
                        emit_ctx_half(h, k0, 0)
                        emit_scores(h, qq, 2 * st)
                        emit_ctx_half(h, k1, 0)
                        emit_scores(h, qq, 2 * st + 1)
                        emit_ctx_half(h, k0, 1)
                        emit_ctx_half(h, k1, 1)
                else:
                    emit_scores(h, qq, 2 * st)
                    emit_scores(h, qq, 2 * st + 1)
                if st == 3 and pend is not None:
                    emit_tail(*pend)
                    pend = None
                emit_exp_pair(h, st)
            for k2 in (TT - 4, TT - 3, TT - 2, TT - 1):
                emit_ctx(h, k2)
            emit_evac(h, qq, u)
            pend = (h, qq, u)
        emit_tail(*pend)

        for c in range(DC):
            for hh in range(2):
                first = (c == 0 and hh == 0)
                em.op("sp", lambda E, c=c, hh=hh: E.dma_start(
                    out=wpstg[:, hh * 512:(hh + 1) * 512],
                    in_=Wproj[c * 128:(c + 1) * 128,
                              hh * 512:(hh + 1) * 512]),
                    outs=[wpsR2[hh]] + ([B_r] if first else []),
                    dma=em.dma_sem(f"wps{hh}"))
                em.op("dve", lambda E, c=c, hh=hh: E.tensor_copy(
                    wp(c)[:, hh * 512:(hh + 1) * 512],
                    wpstg[:, hh * 512:(hh + 1) * 512]),
                    ins=[wpsR2[hh], B_r], outs=[C_r])
        dma("ldbp", bproj_sb[:], bproj_in, ins=[C_r], outs=[bproj_r])
        em.op("dve", lambda E: E.tensor_copy(bprojb[:], bproj_sb[:]),
              ins=[bproj_r], outs=[bprojb_r])
        dma("ldb2", xn1[0:1, 0:D], b2_in, outs=[xn1_r])
        emit_quant(em, "ANTQ_E4M3", b2q[:], xn1[0:1, 0:D],
                   [xn1_r, C_r], [b2q_r], scr[0:1, 0:D], scr_r)
        em.op("dve", lambda E: E.tensor_copy(b2qb[:], b2q[:]),
              ins=[b2q_r], outs=[b2qb_r])

        for tc in range(4):
            t0 = tc * 512
            for tl in range(4):
                ts = tc * 4 + tl
                po = 0
                for c in range(DC):
                    for nb_ in range(2):
                        em.op("pe", lambda E, c=c, nb_=nb_, ts=ts, po=po:
                              E.matmul(
                                  pp[:, po + nb_ * 512:po + (nb_ + 1) * 512],
                                  ctx_all[:, c * T + ts * 128:
                                          c * T + (ts + 1) * 128],
                                  wp(c)[:, nb_ * 512:(nb_ + 1) * 512],
                                  start=(c == 0), stop=False),
                              ins=[W_r, C_r],
                              outs=pb(po + nb_ * 512, po + (nb_ + 1) * 512))
                for nb_ in range(2):
                    em.op("pe", lambda E, nb_=nb_, po=po: E.matmul(
                        pp[:, po + nb_ * 512:po + (nb_ + 1) * 512],
                        onesb1[0:1, 0:128],
                        bprojb[0:1, nb_ * 512:(nb_ + 1) * 512],
                        start=False, stop=True),
                        ins=[onesb1_r, bprojb_r],
                        outs=pb(po + nb_ * 512, po + (nb_ + 1) * 512))
                dma("xinp", xt[:], x_in[b, ts * 128:(ts + 1) * 128, :],
                    outs=[xt_r])
                em.op("dve", lambda E, ts=ts, po=po: E.tensor_tensor(
                    x2sl(ts), pp[:, po:po + D], xt[:], ALU.add),
                    ins=pb(po, po + D) + [xt_r, FA_r], outs=[x2R[ts % 4]])
                emit_ln(x2sl(ts), [x2R[ts % 4], FA_r], xn1[:], xn1_r,
                        xn1[:], xn1_r, ts % 2)
                emit_quant(em, "ANTQ_E4M3",
                           xn2x4[:, tl * 1024:(tl + 1) * 1024], xn1[:],
                           [xn1_r], [xn2x4_r], scr[:], scr_r)
            for tl in range(4):
                for c in range(DC):
                    em.op("pe", lambda E, c=c, tl=tl: E.transpose(
                        ppb[:, c * 128:(c + 1) * 128],
                        xn2x4[:, tl * 1024 + c * 128:tl * 1024 + (c + 1) * 128],
                        ident[:]),
                        ins=[xn2x4_r, ident_r], outs=[ppb_r])
                xview = xn2qT8.rearrange("p (r c) -> p r c", c=512)[
                    :, :, tl * 128:(tl + 1) * 128]
                em.op("dve", lambda E, xview=xview: E.tensor_copy(
                    xview, ppb[:, 0:1024].rearrange("p (r c) -> p r c", c=128)),
                    ins=[ppb_r], outs=[A_r])

            def w1_stream(quarter):
                slq = quarter % 2
                for c in range(DC):
                    em.op("sp", lambda E, c=c, quarter=quarter, slq=slq:
                          E.dma_start(
                              out=w1q8[slq][:, c * 1024:(c + 1) * 1024],
                              in_=w1q_sc[c * 128:(c + 1) * 128,
                                         quarter * 1024:(quarter + 1) * 1024]),
                          ins=[spillR1],
                          outs=[W1R[slq]],
                          dma=em.dma_sem(f"w1s{(2 * slq + c % 2)}"))

            if tc == 0:
                w1_stream(0)
            for quarter in range(4):
                if quarter + 1 < 4:
                    w1_stream(quarter + 1)
                elif tc + 1 < 4:
                    w1_stream(0)
                slq = quarter % 2
                w1v = w1q8[slq].rearrange("p (c q) -> p c q", q=1024)
                x8v = xn2qT8.rearrange("p (c t) -> p c t", t=512)
                for j in range(8):
                    Ht = quarter * 8 + j
                    hb = 2048 + 512 * (j % 2)
                    for c2 in range(DC // 2):
                        em.op("pe", lambda E, c2=c2, j=j, hb=hb, w1v=w1v,
                              x8v=x8v: E.matmul(
                                  pp[:, hb:hb + 512],
                                  w1v[:, 2 * c2:2 * c2 + 2,
                                      j * 128:(j + 1) * 128],
                                  x8v[:, 2 * c2:2 * c2 + 2, :],
                                  start=(c2 == 0), stop=(c2 == DC // 2 - 1),
                                  perf_mode=DR),
                              ins=[W1R[slq], A_r], outs=pb(hb, hb + 512))
                    em.op("act", lambda E, hb=hb, Ht=Ht: E.activation(
                        xn1[:, 0:512], pp[:, hb:hb + 512], AF.Relu,
                        bias=b1q[:, Ht:Ht + 1], scale=1.0),
                        ins=pb(hb, hb + 512) + [b1q_r], outs=[xn1_r])
                    emit_quant(em, "ANTQ_FP16", yfull[:, 0:512],
                               xn1[:, 0:512], [xn1_r], [yfull_r],
                               scr[:, 0:512], scr_r)
                    emit_quant(em, "ANTQ_E4M3",
                               hTbf8[:, Ht * 512:(Ht + 1) * 512],
                               yfull[:, 0:512], [yfull_r, A_r], [B_r],
                               scr[:, 512:1024], scr_r)

            hTv = hTbf8.rearrange("p (H t) -> p H t", t=512)
            for tg in range(2):
                for Hp in range(HT // 2):
                    wbv = w2bf8[Hp % 4]
                    wbrL = w2bR2[Hp % 4]
                    wbj = wbv[:].rearrange("p (j m) -> p j m", j=2)
                    for hh in range(2):
                        em.op("sp", lambda E, Hp=Hp, wbj=wbj, hh=hh:
                              E.dma_start(
                                  out=wbj[:, :, hh * 512:(hh + 1) * 512],
                                  in_=w2q_sc[2 * Hp * 128:(2 * Hp + 2) * 128,
                                             hh * 512:(hh + 1) * 512]
                                  .rearrange("(j p) m -> p j m", j=2)),
                              ins=[C_r, spillR], outs=[wbrL[hh]],
                              dma=em.dma_sem(f"w2s{(Hp % 4) * 2 + hh}"))
                    for tl2 in range(2):
                        tl = tg * 2 + tl2
                        yb = 1024 + tl2 * 1024
                        for nb_ in range(2):
                            em.op("pe", lambda E, Hp=Hp, tl=tl, yb=yb,
                                  nb_=nb_, wbj=wbj: E.matmul(
                                      pp[:, yb + nb_ * 512:
                                         yb + (nb_ + 1) * 512],
                                      hTv[:, 2 * Hp:2 * Hp + 2,
                                          tl * 128:(tl + 1) * 128],
                                      wbj[:, :, nb_ * 512:(nb_ + 1) * 512],
                                      start=(Hp == 0), stop=False,
                                      perf_mode=DR),
                                  ins=[B_r, wbrL[nb_]],
                                  outs=pb(yb + nb_ * 512,
                                          yb + (nb_ + 1) * 512))
                for tl2 in range(2):
                    tl = tg * 2 + tl2
                    ts = tc * 4 + tl
                    yb = 1024 + tl2 * 1024
                    for nb_ in range(2):
                        em.op("pe", lambda E, yb=yb, nb_=nb_: E.matmul(
                            pp[:, yb + nb_ * 512:yb + (nb_ + 1) * 512],
                            onesb1[0:1, 0:128],
                            b2qb[0:1, nb_ * 512:(nb_ + 1) * 512],
                            start=False, stop=True),
                            ins=[onesb1_r, b2qb_r],
                            outs=pb(yb + nb_ * 512, yb + (nb_ + 1) * 512))
                    dma("mkin", mk_sb[:], mask_in[b, ts * 128:(ts + 1) * 128, :],
                        outs=[mk_sb_r])
                    emit_quant(em, "ANTQ_FP16", yq[:], pp[:, yb:yb + 1024],
                               pb(yb, yb + 1024) + [A_r], [yq_r],
                               scr[:], scr_r)
                    em.op("dve", lambda E: E.tensor_tensor(
                        yq[:], yq[:], mk_sb[:], ALU.mult),
                        ins=[yq_r, mk_sb_r], outs=[yq_r])
                    em.op("dve", lambda E, ts=ts: E.tensor_tensor(
                        x2sl(ts), x2sl(ts), yq[:], ALU.add),
                        ins=[x2R[ts % 4], yq_r, FA_r], outs=[x2R[ts % 4]])
                    em.op("sp", lambda E, ts=ts: E.dma_start(
                        out=out_hbm[b, ts * 128:(ts + 1) * 128, :],
                        in_=x2sl(ts)),
                        ins=[x2R[ts % 4], FA_r],
                        dma=em.dma_sem(f"oout{ts % 4}"))

    for s in ("oout0", "oout1", "oout2", "oout3"):
        if s in em.cnt:
            em.wait("sp", s, em.cnt[s])
    em.flush()
    em.close()


_PROG_CACHE = {}


def _get_program():
    if "nc" not in _PROG_CACHE:
        nc = bacc.Bacc("TRN2", target_bir_lowering=False, debug=False)
        build_program(nc)
        nc.finalize()
        _PROG_CACHE["nc"] = nc
    return _PROG_CACHE["nc"]


def _rotl32(x, d):
    return ((x << np.uint32(d)) | (x >> np.uint32(32 - d))).astype(np.uint32)


def _threefry2x32(k0, k1, x0, x1):
    x0 = x0.astype(np.uint32).copy()
    x1 = x1.astype(np.uint32).copy()
    ks0 = np.uint32(k0)
    ks1 = np.uint32(k1)
    ks2 = np.uint32(ks0 ^ ks1 ^ np.uint32(0x1BD11BDA))
    rot = [(13, 15, 26, 6), (17, 29, 16, 24)]
    x0 = (x0 + ks0).astype(np.uint32)
    x1 = (x1 + ks1).astype(np.uint32)
    ks = [ks0, ks1, ks2]
    for i in range(5):
        for d in rot[i % 2]:
            x0 = (x0 + x1).astype(np.uint32)
            x1 = _rotl32(x1, d)
            x1 = (x1 ^ x0).astype(np.uint32)
        x0 = (x0 + ks[(i + 1) % 3]).astype(np.uint32)
        x1 = (x1 + ks[(i + 2) % 3] + np.uint32(i + 1)).astype(np.uint32)
    return x0, x1


def _dropout_keep_mask(n):
    i64 = np.arange(n, dtype=np.uint64)
    hi = (i64 >> np.uint64(32)).astype(np.uint32)
    lo = (i64 & np.uint64(0xFFFFFFFF)).astype(np.uint32)
    a, b = _threefry2x32(np.uint32(0), np.uint32(42), hi, lo)
    if os.environ.get("BASS_SIM_FP64") == "1":
        bits64 = (a.astype(np.uint64) << np.uint64(32)) | b.astype(np.uint64)
        fb = (bits64 >> np.uint64(12)) | np.uint64(0x3FF0000000000000)
        u = fb.view(np.float64) - np.float64(1.0)
        return u < np.float64(0.8)
    bits = a ^ b
    fb = (bits >> np.uint32(9)) | np.uint32(0x3F800000)
    u = fb.view(np.float32) - np.float32(1.0)
    return u < np.float32(0.8)


def kernel(**inputs):
    import ml_dtypes

    x = np.ascontiguousarray(np.asarray(inputs["L0_attn_x"], dtype=np.float32))
    gamma = np.asarray(inputs["L0_attn_ln_gamma"], dtype=np.float32)
    beta = np.asarray(inputs["L0_attn_ln_beta"], dtype=np.float32)
    Wq = np.ascontiguousarray(np.asarray(inputs["L0_attn_W_qkv"], np.float32))
    bq = np.asarray(inputs["L0_attn_b_qkv"], np.float32).reshape(1, 3 * D)
    Wp = np.ascontiguousarray(np.asarray(inputs["L0_attn_W_proj"], np.float32))
    bp = np.asarray(inputs["L0_attn_b_proj"], np.float32).reshape(1, D)
    w1 = np.asarray(inputs["L0_ffn_w1"], np.float32)
    b1 = np.asarray(inputs["L0_ffn_b1"], np.float32).reshape(HT, 128)
    w2 = np.asarray(inputs["L0_ffn_w2"], np.float32)
    b2 = np.asarray(inputs["L0_ffn_b2"], np.float32).reshape(1, D)

    w1t = np.ascontiguousarray(w1.T)
    w2t = np.ascontiguousarray(w2.T)
    gam = np.ascontiguousarray(gamma.reshape(DC, 128))
    bet = np.ascontiguousarray(beta.reshape(DC, 128))
    idn = np.eye(128, dtype=np.float32).astype(ml_dtypes.bfloat16)
    idf = np.eye(128, dtype=np.float32)
    cstv = _make_consts()

    keep = _dropout_keep_mask(NCORES * NB * T * D).reshape(NCORES * NB, T, D)
    mk = np.where(keep, np.float32(1.25), np.float32(0.0)).astype(
        ml_dtypes.bfloat16)

    nc = _get_program()
    shared = dict(wqkv=Wq, bqkv=bq, wproj=Wp, bproj=bp, w1t=w1t, b1=b1,
                  w2t=w2t, b2=b2, gam=gam, bet=bet, idn=idn, idf=idf,
                  cst=cstv)
    in_maps = []
    for i in range(NCORES):
        m = dict(shared)
        m["x"] = np.ascontiguousarray(x[NB * i:NB * (i + 1)])
        m["mk"] = np.ascontiguousarray(mk[NB * i:NB * (i + 1)])
        in_maps.append(m)
    res = run_bass_kernel_spmd(nc, in_maps, core_ids=list(range(NCORES)))
    out = np.concatenate(
        [np.asarray(r["out"], dtype=np.float32) for r in res.results], axis=0)
    return out
